# version 01
# baseline (speedup 1.0000x reference)
import numpy as np
from contextlib import ExitStack
import concourse.bass as bass
import concourse.mybir as mybir
from concourse.bass_utils import run_bass_kernel_spmd

F32 = mybir.dt.float32
BF16 = mybir.dt.bfloat16
I32 = mybir.dt.int32
AF = mybir.ActivationFunctionType
ALU = mybir.AluOpType
AX = mybir.AxisListType

D = 1024
T = 2048
DB = D // 128
RND = 12582912.0


class Sem:
    __slots__ = ("h", "n")
    def __init__(self, h):
        self.h = h
        self.n = 0
    def inc(self, instr, k=16):
        instr.then_inc(self.h, k)
        self.n += k
        return instr

_EMITTING = {
    "tensor_copy", "tensor_tensor", "tensor_add", "tensor_sub", "tensor_mul",
    "tensor_max", "tensor_scalar", "tensor_single_scalar", "tensor_scalar_add",
    "tensor_scalar_sub", "tensor_scalar_mul", "tensor_scalar_max",
    "tensor_scalar_min", "tensor_relu", "tensor_reduce", "reduce_max",
    "reduce_sum", "scalar_tensor_tensor", "tensor_tensor_scan", "memset",
    "memzero", "reciprocal", "transpose", "select", "copy_predicated",
    "activation", "sign", "square", "sqrt", "mul", "add", "copy", "iota",
    "affine_select", "bn_stats", "bn_aggr", "max", "stream_shuffle",
}

class EW:
    def __init__(self, ctx, eng, name):
        self._eng = eng
        self.sem = ctx.S("auto_" + name)
    def __getattr__(self, a):
        eng = self.__dict__["_eng"]
        sem = self.__dict__["sem"]
        f = getattr(eng, a)
        if a in _EMITTING:
            def g(*args, **kw):
                eng.wait_ge(sem.h, sem.n)
                ii = f(*args, **kw)
                sem.inc(ii, 1)
                return ii
            return g
        return f

class Ctx:
    def __init__(self, nc, stack):
        self.nc = nc
        self.stack = stack
        self.sems = {}
        self.bufs = {}
        self.V = EW(self, nc.vector, "V")
        self.SC = EW(self, nc.scalar, "S")
        self.GP = EW(self, nc.gpsimd, "G")
    def S(self, name):
        if name not in self.sems:
            self.sems[name] = Sem(self.stack.enter_context(self.nc.semaphore(name)))
        return self.sems[name]
    def sbuf(self, name, shape, dtype=F32):
        if name not in self.bufs:
            self.bufs[name] = self.stack.enter_context(self.nc.sbuf_tensor(name, shape, dtype))
        return self.bufs[name]
    def psum(self, name, shape, dtype=F32):
        if name not in self.bufs:
            self.bufs[name] = self.stack.enter_context(self.nc.psum_tensor(name, shape, dtype))
        return self.bufs[name]
    def wV(self, eng):
        eng.wait_ge(self.V.sem.h, self.V.sem.n)
    def wS(self, eng):
        eng.wait_ge(self.SC.sem.h, self.SC.sem.n)

def x_re(x_dram, nt=T):
    return bass.AP(x_dram, 0, [[nt, 128], [128 * nt, DB], [1, nt]])

def x_chunk_ap(x_dram, t0, w, nt):
    return bass.AP(x_dram, t0, [[nt, 128], [128 * nt, DB], [1, w]])

def sb3(t, w):
    return t[:, :].rearrange("p (b t) -> p b t", t=w)


def emit_ln(ctx, xs, xs_off, xs_nt, xn, xn_off, xn_stp, pad, nt,
            gamma=None, beta=None, t_base=0, src_bf16=False):
    nc = ctx.nc
    V, SC = ctx.V, ctx.SC
    onesc = ctx.sbuf("onesc", [128, 1])
    onescb = ctx.sbuf("onescb", [128, 1], BF16)
    ones1 = ctx.sbuf("ones1", [1, 128])
    sq = ctx.sbuf("lnsq", [128, 2 * 512])
    rows = ctx.sbuf("lnrow", [1, 3 * 512])
    rb = ctx.sbuf("lnrb", [128, 1024])
    pA = ctx.psum("lnA", [128, 512])
    pB = ctx.psum("lnB", [128, 512])
    sT = ctx.S("lnT")
    oc = onescb if src_bf16 else onesc
    ctx.wV(nc.scalar)
    for c in range((nt + 511) // 512):
        t0 = t_base + c * 512
        w = min(512, t_base + nt - t0)
        ctx.wV(nc.tensor)
        for b in range(DB):
            nc.tensor.matmul(pB[0:1, 0:w], oc[:, 0:1],
                             xs[:, xs_off + b * xs_nt + t0: xs_off + b * xs_nt + t0 + w],
                             start=(b == 0), stop=(b == DB - 1))
        tcnt = {}
        for b in range(DB):
            pp = b % 2
            if b >= 2:
                nc.scalar.wait_ge(sT.h, tcnt[b - 2])
            SC.activation(sq[:, pp * 512: pp * 512 + w],
                          xs[:, xs_off + b * xs_nt + t0: xs_off + b * xs_nt + t0 + w],
                          AF.Square)
            ctx.wS(nc.tensor)
            ii = nc.tensor.matmul(pA[0:1, 0:w], onesc[:, 0:1], sq[:, pp * 512: pp * 512 + w],
                                  start=(b == 0), stop=(b == DB - 1))
            sT.inc(ii, 1)
            tcnt[b] = sT.n
        m = rows[0:1, 0:512]; v = rows[0:1, 512:1024]; r = rows[0:1, 1024:1536]
        nc.vector.wait_ge(sT.h, sT.n)
        V.tensor_scalar(m[:, 0:w], pB[0:1, 0:w], 1.0 / D, None, ALU.mult)
        V.tensor_mul(v[:, 0:w], m[:, 0:w], m[:, 0:w])
        V.tensor_scalar(v[:, 0:w], v[:, 0:w], -1.0, 1e-5, ALU.mult, ALU.add)
        V.tensor_scalar(r[:, 0:w], pA[0:1, 0:w], 1.0 / D, None, ALU.mult)
        V.tensor_add(v[:, 0:w], v[:, 0:w], r[:, 0:w])
        ctx.wV(nc.scalar)
        SC.activation(r[:, 0:w], v[:, 0:w], AF.Ln)
        SC.activation(r[:, 0:w], r[:, 0:w], AF.Exp, scale=-0.5)
        ctx.wS(nc.vector)
        V.tensor_mul(m[:, 0:w], m[:, 0:w], r[:, 0:w])
        ctx.wV(nc.tensor)
        nc.tensor.matmul(pB[:, 0:w], ones1[0:1, :], r[:, 0:w], start=True, stop=True)
        ii = nc.tensor.matmul(pA[:, 0:w], ones1[0:1, :], m[:, 0:w], start=True, stop=True)
        sT.inc(ii, 1)
        nc.vector.wait_ge(sT.h, sT.n)
        V.tensor_copy(rb[:, 0:w], pB[:, 0:w])
        V.tensor_copy(rb[:, 512:512 + w], pA[:, 0:w])
        for b in range(DB):
            dst = xn[:, xn_off + b * xn_stp + pad + t0: xn_off + b * xn_stp + pad + t0 + w]
            src = xs[:, xs_off + b * xs_nt + t0: xs_off + b * xs_nt + t0 + w]
            V.tensor_mul(dst, src, rb[:, 0:w])
            V.tensor_sub(dst, dst, rb[:, 512:512 + w])
            if gamma is not None:
                V.tensor_scalar(dst, dst, gamma[:, b:b + 1], beta[:, b:b + 1],
                                ALU.mult, ALU.add)


def build_l0a():
    nc = bass.Bass("TRN2", target_bir_lowering=False, detect_race_conditions=False)
    stack = ExitStack()
    def I(name, shape, dt=F32):
        return nc.dram_tensor(name, shape, dt, kind="ExternalInput")
    x_d = I("x", [D, T]); xp_d = I("xp", [D, T])
    halo_d = I("halo", [D, 2])
    vmask_d = I("vmask", [32, 1])
    gw1_d = I("gw1T", [D, 256]); gb1_d = I("gb1", [128, 2])
    gw2_d = I("gw2T", [256, 32]); gb2_d = I("gb2", [32, 1])
    uw_d = I("uw", [D, 32]); vw_d = I("vw", [D, 32])
    cw_d = I("cw", [128, 24]); ow_d = I("owT", [32, D]); ob_d = I("ob", [128, 8])
    xo_d = nc.dram_tensor("xo", [D, T], F32, kind="ExternalOutput")

    with stack:
        ctx = Ctx(nc, stack)
        V, SC = ctx.V, ctx.SC
        TP = T + 2
        xs = ctx.sbuf("xs", [128, DB * T])
        xn = ctx.sbuf("xn", [128, DB * TP], BF16)
        xpc = ctx.sbuf("xpc", [128, DB * 512])
        xnp = ctx.sbuf("xnp", [128, DB * 512], BF16)
        haloin = ctx.sbuf("haloin", [128, 2 * DB])
        halon = ctx.sbuf("halon", [128, 2 * DB], BF16)
        w_g1 = ctx.sbuf("w_g1", [128, DB * 256], BF16)
        w_g2 = ctx.sbuf("w_g2", [128, 2 * 32], BF16)
        w_u = ctx.sbuf("w_u", [128, DB * 32], BF16)
        w_v = ctx.sbuf("w_v", [128, DB * 32], BF16)
        w_ow = ctx.sbuf("w_ow", [32, D], BF16)
        cwt = ctx.sbuf("cwt_s", [128, 24])
        gb1 = ctx.sbuf("gb1_s", [128, 2])
        gb2 = ctx.sbuf("gb2_s", [32, 1])
        ob = ctx.sbuf("ob_s", [128, 8])
        vmask = ctx.sbuf("vmask_s", [32, 1])
        hgate = ctx.sbuf("hgate", [128, 2 * T], BF16)
        Ub = ctx.sbuf("Ub", [32, T])
        Vb = ctx.sbuf("Vb", [32, T])
        Sb = ctx.sbuf("Sb", [32, T])
        onesT = ctx.sbuf("onesT", [32, T])
        gate = ctx.sbuf("gate", [32, T], BF16)
        glob = ctx.sbuf("glob", [32, T], BF16)
        spb = ctx.sbuf("spb", [128, DB])
        tmpc = ctx.sbuf("tmpc", [128, DB])
        spbh = ctx.sbuf("spbh", [128, DB], BF16)
        voff = ctx.sbuf("voff", [32, 1])
        onesc = ctx.sbuf("onesc", [128, 1])
        ones1 = ctx.sbuf("ones1", [1, 128])
        p0 = ctx.psum("p0", [128, 512]); p1 = ctx.psum("p1", [128, 512])
        p2 = ctx.psum("p2", [128, 512]); p3 = ctx.psum("p3", [128, 512])
        ld, st, ldg = ctx.S("ld"), ctx.S("st"), ctx.S("ldg")
        sT = ctx.S("mT")

        with nc.Block():
            ld.inc(nc.sync.dma_start(sb3(xs, T), x_re(x_d)))
            ld.inc(nc.sync.dma_start(haloin[:, :].rearrange("p (b o) -> p b o", b=DB),
                                     bass.AP(halo_d, 0, [[2, 128], [256, DB], [1, 2]])))
            ld.inc(nc.sync.dma_start(cwt[:, :], cw_d[:, :]))
            ld.inc(nc.sync.dma_start(gb1[:, :], gb1_d[:, :]))
            ld.inc(nc.sync.dma_start(gb2[:, :], gb2_d[:, :]))
            ld.inc(nc.sync.dma_start(ob[:, :], ob_d[:, :]))
            ld.inc(nc.sync.dma_start(vmask[:, :], vmask_d[:, :]))
            ldg.inc(nc.gpsimd.dma_start(w_g1[:, :].rearrange("p (b g) -> p b g", b=DB),
                                        bass.AP(gw1_d, 0, [[256, 128], [128 * 256, DB], [1, 256]])))
            ldg.inc(nc.gpsimd.dma_start(w_g2[:, :].rearrange("p (b g) -> p b g", b=2),
                                        bass.AP(gw2_d, 0, [[32, 128], [128 * 32, 2], [1, 32]])))
            ldg.inc(nc.gpsimd.dma_start(w_u[:, :].rearrange("p (b g) -> p b g", b=DB),
                                        bass.AP(uw_d, 0, [[32, 128], [128 * 32, DB], [1, 32]])))
            ldg.inc(nc.gpsimd.dma_start(w_v[:, :].rearrange("p (b g) -> p b g", b=DB),
                                        bass.AP(vw_d, 0, [[32, 128], [128 * 32, DB], [1, 32]])))
            ldg.inc(nc.gpsimd.dma_start(w_ow[:, :], ow_d[:, :]))
            V.memset(onesc[:, :], 1.0)
            V.memset(ones1[:, :], 1.0)
            V.memset(onesT[:, :], 1.0)
            for e in (nc.tensor, nc.vector, nc.scalar):
                e.wait_ge(ld.h, ld.n)
                e.wait_ge(ldg.h, ldg.n)
            ctx.wV(nc.tensor)

            emit_ln(ctx, xs, 0, T, xn, 0, TP, 1, T)
            emit_ln(ctx, haloin, 0, 2, halon, 0, 2, 0, 2)
            for b in range(DB):
                V.tensor_copy(xn[:, b * TP: b * TP + 1], halon[:, 2 * b: 2 * b + 1])
                V.tensor_copy(xn[:, b * TP + TP - 1: b * TP + TP], halon[:, 2 * b + 1: 2 * b + 2])

            V.memset(spb[:, :], 0.0)
            for c in range(4):
                t0 = c * 512
                ctx.wV(nc.sync)
                ld.inc(nc.sync.dma_start(sb3(xpc, 512), x_chunk_ap(xp_d, t0, 512, T)))
                for e in (nc.tensor, nc.vector, nc.scalar):
                    e.wait_ge(ld.h, ld.n)
                emit_ln(ctx, xpc, 0, 512, xnp, 0, 512, 0, 512)
                for b in range(DB):
                    V.reduce_sum(tmpc[:, b:b + 1], xnp[:, b * 512:(b + 1) * 512], axis=AX.X)
                V.tensor_add(spb[:, :], spb[:, :], tmpc[:, :])
            V.tensor_copy(spbh[:, :], spb[:, :])
            ctx.wV(nc.tensor)
            for b in range(DB):
                ii = nc.tensor.matmul(p2[0:32, 0:1], w_v[:, b * 32:(b + 1) * 32], spbh[:, b:b + 1],
                                      start=(b == 0), stop=(b == DB - 1))
            sT.inc(ii, 1)
            nc.vector.wait_ge(sT.h, sT.n)
            V.tensor_copy(voff[:, :], p2[0:32, 0:1])
            V.tensor_mul(voff[:, :], voff[:, :], vmask[:, :])

            for tc in range(4):
                t0 = tc * 512
                for gc in range(2):
                    pt = (p0, p1)[(tc * 2 + gc) % 2]
                    ctx.wS(nc.tensor)
                    for b in range(DB):
                        ii = nc.tensor.matmul(pt[:, 0:512],
                                              w_g1[:, b * 256 + gc * 128: b * 256 + gc * 128 + 128],
                                              xn[:, b * TP + 1 + t0: b * TP + 1 + t0 + 512],
                                              start=(b == 0), stop=(b == DB - 1))
                    sT.inc(ii, 1)
                    nc.scalar.wait_ge(sT.h, sT.n)
                    SC.activation(hgate[:, gc * T + t0: gc * T + t0 + 512],
                                  pt[:, 0:512], AF.Relu, bias=gb1[:, gc:gc + 1])
            for tc in range(4):
                t0 = tc * 512
                pt = (p0, p1)[tc % 2]
                ctx.wS(nc.tensor)
                for gc in range(2):
                    ii = nc.tensor.matmul(pt[0:32, 0:512], w_g2[:, gc * 32: gc * 32 + 32],
                                          hgate[:, gc * T + t0: gc * T + t0 + 512],
                                          start=(gc == 0), stop=(gc == 1))
                sT.inc(ii, 1)
                nc.scalar.wait_ge(sT.h, sT.n)
                SC.activation(gate[:, t0: t0 + 512], pt[0:32, 0:512],
                              AF.Sigmoid, bias=gb2[:, 0:1])
            for tc in range(4):
                t0 = tc * 512
                ctx.wV(nc.tensor)
                for b in range(DB):
                    nc.tensor.matmul(p2[0:32, 0:512], w_u[:, b * 32:(b + 1) * 32],
                                     xn[:, b * TP + 1 + t0: b * TP + 1 + t0 + 512],
                                     start=(b == 0), stop=(b == DB - 1))
                for b in range(DB):
                    ii = nc.tensor.matmul(p3[0:32, 0:512], w_v[:, b * 32:(b + 1) * 32],
                                          xn[:, b * TP + 1 + t0: b * TP + 1 + t0 + 512],
                                          start=(b == 0), stop=(b == DB - 1))
                sT.inc(ii, 1)
                nc.vector.wait_ge(sT.h, sT.n)
                V.tensor_copy(Ub[:, t0:t0 + 512], p2[0:32, 0:512])
                V.tensor_copy(Vb[:, t0:t0 + 512], p3[0:32, 0:512])
            V.tensor_tensor_scan(Sb[:, :], onesT[:, :], Vb[:, :], voff[:, 0:1],
                                 ALU.mult, ALU.add)
            V.tensor_mul(Ub[:, :], Ub[:, :], Sb[:, :])
            V.tensor_mul(glob[:, :], Ub[:, :], gate[:, :])
            ctx.wS(nc.vector)
            for b in range(DB):
                xv = xs[:, b * T:(b + 1) * T]
                mid = xn[:, b * TP + 1: b * TP + 1 + T]
                left = xn[:, b * TP: b * TP + T]
                right = xn[:, b * TP + 2: b * TP + 2 + T]
                V.tensor_scalar(xv, xv, ob[:, b:b + 1], None, ALU.add)
                V.scalar_tensor_tensor(xv, mid, cwt[:, b * 3 + 1: b * 3 + 2], xv, ALU.mult, ALU.add)
                V.scalar_tensor_tensor(xv, left, cwt[:, b * 3 + 0: b * 3 + 1], xv, ALU.mult, ALU.add)
                V.scalar_tensor_tensor(xv, right, cwt[:, b * 3 + 2: b * 3 + 3], xv, ALU.mult, ALU.add)
            for dc in range(DB):
                for tc in range(4):
                    t0 = tc * 512
                    pt = (p0, p1)[(dc * 4 + tc) % 2]
                    ctx.wV(nc.tensor)
                    ii = nc.tensor.matmul(pt[:, 0:512], w_ow[:, dc * 128:(dc + 1) * 128],
                                          glob[:, t0:t0 + 512], start=True, stop=True)
                    sT.inc(ii, 1)
                    nc.vector.wait_ge(sT.h, sT.n)
                    xv = xs[:, dc * T + t0: dc * T + t0 + 512]
                    V.tensor_add(xv, xv, pt[:, 0:512])
            ctx.wV(nc.sync)
            st.inc(nc.sync.dma_start(x_re(xo_d), sb3(xs, T)))
            nc.sync.wait_ge(st.h, st.n)
    return nc

def prep_l0a(full, b, r):
    xc = full["x_cur"]
    x = np.ascontiguousarray(xc[b, r * T:(r + 1) * T, :].T)
    xp = np.ascontiguousarray(xc[b, (1 - r) * T:(2 - r) * T, :].T)
    zc = np.zeros((D, 1), np.float32)
    halo_lo = np.ascontiguousarray(xc[b, T - 1: T, :].T) if r == 1 else zc
    halo_hi = np.ascontiguousarray(xc[b, T: T + 1, :].T) if r == 0 else zc
    return {
        "x": x, "xp": xp,
        "halo": np.ascontiguousarray(np.concatenate([halo_lo, halo_hi], 1)),
        "vmask": np.full((32, 1), float(r), np.float32),
        "gw1T": np.ascontiguousarray(full["L0_attn_gate_w1"]),
        "gb1": np.ascontiguousarray(full["L0_attn_gate_b1"].reshape(2, 128).T),
        "gw2T": np.ascontiguousarray(full["L0_attn_gate_w2"]),
        "gb2": np.ascontiguousarray(full["L0_attn_gate_b2"].reshape(32, 1)),
        "uw": np.ascontiguousarray(full["L0_attn_U_w"]),
        "vw": np.ascontiguousarray(full["L0_attn_V_w"]),
        "cw": np.ascontiguousarray(full["L0_attn_conv_w"].reshape(8, 128, 3).transpose(1, 0, 2).reshape(128, 24)),
        "owT": np.ascontiguousarray(full["L0_attn_out_w"]),
        "ob": np.ascontiguousarray(full["L0_attn_out_b"].reshape(8, 128).T),
    }

def gather_x(results, key="xo"):
    out = np.zeros((4, 4096, 1024), np.float32)
    for c in range(8):
        b, r = c // 2, c % 2
        out[b, r * T:(r + 1) * T, :] = np.asarray(results[c][key]).T
    return out


def simulate_one(nc, in_map, out_names):
    from concourse.bass_interp import CoreSim
    sim = CoreSim(nc, require_finite=False, require_nnan=False)
    for k, v in in_map.items():
        view = sim.tensor(k)
        view[:] = v.reshape(view.shape)
    sim.simulate()
    ex = sim.instruction_executor
    return {n: np.array(ex.mems[n].view(dtype=mybir.dt.np(ex.mem_default_dtypes[n])))
            for n in out_names}

def build_l0f():
    nc = bass.Bass("TRN2", target_bir_lowering=False, detect_race_conditions=False)
    stack = ExitStack()
    x_d = nc.dram_tensor("x", [D, T], F32, kind="ExternalInput")
    w1_d = nc.dram_tensor("w1T", [D, 4096], F32, kind="ExternalInput")
    b1_d = nc.dram_tensor("b1t", [128, 32], F32, kind="ExternalInput")
    w2_d = nc.dram_tensor("w2T", [4096, D], F32, kind="ExternalInput")
    b2_d = nc.dram_tensor("b2t", [128, 8], F32, kind="ExternalInput")
    xo_d = nc.dram_tensor("xo", [D, T], F32, kind="ExternalOutput")
    w2scr = nc.dram_tensor("w2scr", [4096, D], BF16, kind="Internal")

    FF = 4096
    NFB = FF // 128
    with stack:
        ctx = Ctx(nc, stack)
        V, SC = ctx.V, ctx.SC
        xn = ctx.sbuf("xn", [128, DB * T], BF16)
        w1t = ctx.sbuf("w1t", [128, DB * FF], BF16)
        wchunk = ctx.sbuf("wchunk", [128, FF], F32)
        h = ctx.sbuf("h", [128, NFB * 512], BF16)
        xre = ctx.sbuf("xre", [128, DB * 512], F32)
        w2b = [ctx.sbuf(f"w2b{i}", [128, D], BF16) for i in range(2)]
        nxc = ctx.sbuf("nxc", [128, DB * 512], F32)
        b1t = ctx.sbuf("b1t_s", [128, 32])
        b2t = ctx.sbuf("b2t_s", [128, 8])
        absacc = ctx.sbuf("absacc", [128, 2])
        abst = ctx.sbuf("abst", [128, 1])
        tmp8 = ctx.sbuf("tmp8", [128, 8])
        rowsc = ctx.sbuf("rowsc", [1, 4])
        rcol = ctx.sbuf("rcol", [128, 2])
        onesc = ctx.sbuf("onesc", [128, 1])
        ones1 = ctx.sbuf("ones1", [1, 128])
        p0 = ctx.psum("p0", [128, 512]); p1 = ctx.psum("p1", [128, 512])
        p2 = ctx.psum("p2", [128, 512]); p3 = ctx.psum("p3", [128, 512])
        p4 = ctx.psum("p4", [128, 512]); p5 = ctx.psum("p5", [128, 512])
        ld, st = ctx.S("ld"), ctx.S("st")
        sT = ctx.S("mT")

        with nc.Block():
            V.memset(onesc[:, :], 1.0)
            V.memset(ones1[:, :], 1.0)
            V.memset(absacc[:, :], 0.0)
            ld.inc(nc.sync.dma_start(b1t[:, :], b1_d[:, :]))
            ld.inc(nc.sync.dma_start(b2t[:, :], b2_d[:, :]))
            for tc in range(4):
                t0 = tc * 512
                ctx.wV(nc.sync)
                ctx.wS(nc.sync)
                ld.inc(nc.sync.dma_start(sb3(xre, 512), x_chunk_ap(x_d, t0, 512, T)))
                for e in (nc.tensor, nc.vector, nc.scalar):
                    e.wait_ge(ld.h, ld.n)
                emit_ln(ctx, xre, 0, 512, xn, t0, T, 0, 512)
            for wi, (wd, nchunk) in enumerate([(w1_d, DB), (w2_d, NFB)]):
                nw = 4096 if wi == 0 else D
                for c in range(nchunk):
                    ctx.wV(nc.sync)
                    ld.inc(nc.sync.dma_start(wchunk[:, 0:nw], wd[c * 128:(c + 1) * 128, :]))
                    nc.vector.wait_ge(ld.h, ld.n)
                    for s in range(nw // 512):
                        V.reduce_sum(tmp8[:, s:s + 1], wchunk[:, s * 512:(s + 1) * 512],
                                     axis=AX.X, apply_absolute_value=True)
                    V.reduce_sum(abst[:, 0:1], tmp8[:, 0:nw // 512], axis=AX.X)
                    V.tensor_add(absacc[:, wi:wi + 1], absacc[:, wi:wi + 1], abst[:, 0:1])
            ctx.wV(nc.tensor)
            ii = nc.tensor.matmul(p2[0:1, 0:2], onesc[:, 0:1], absacc[:, 0:2],
                                  start=True, stop=True)
            sT.inc(ii, 1)
            nc.vector.wait_ge(sT.h, sT.n)
            V.tensor_scalar(rowsc[0:1, 0:2], p2[0:1, 0:2], 1.0 / (4096.0 * 1024.0), None, ALU.mult)
            V.tensor_scalar(rowsc[0:1, 0:2], rowsc[0:1, 0:2], 1e-8, None, ALU.max)
            V.reciprocal(rowsc[0:1, 0:2], rowsc[0:1, 0:2])
            ctx.wV(nc.tensor)
            ii = nc.tensor.matmul(p2[:, 0:2], ones1[0:1, :], rowsc[0:1, 0:2],
                                  start=True, stop=True)
            sT.inc(ii, 1)
            nc.vector.wait_ge(sT.h, sT.n)
            V.tensor_copy(rcol[:, 0:2], p2[:, 0:2])
            for c in range(DB):
                ctx.wV(nc.sync)
                ld.inc(nc.sync.dma_start(wchunk[:, :], w1_d[c * 128:(c + 1) * 128, :]))
                nc.vector.wait_ge(ld.h, ld.n)
                V.tensor_scalar(wchunk[:, :], wchunk[:, :], rcol[:, 0:1], None, ALU.mult)
                V.tensor_scalar(wchunk[:, :], wchunk[:, :], RND, None, ALU.add)
                V.tensor_scalar(wchunk[:, :], wchunk[:, :], RND, None, ALU.subtract)
                V.tensor_scalar(wchunk[:, :], wchunk[:, :], 1.0, None, ALU.min)
                V.tensor_scalar(w1t[:, c * FF:(c + 1) * FF], wchunk[:, :], -1.0, None, ALU.max)
            for c in range(NFB):
                ctx.wV(nc.sync)
                nc.sync.wait_ge(st.h, st.n)
                ld.inc(nc.sync.dma_start(wchunk[:, 0:D], w2_d[c * 128:(c + 1) * 128, :]))
                nc.vector.wait_ge(ld.h, ld.n)
                V.tensor_scalar(wchunk[:, 0:D], wchunk[:, 0:D], rcol[:, 1:2], None, ALU.mult)
                V.tensor_scalar(wchunk[:, 0:D], wchunk[:, 0:D], RND, None, ALU.add)
                V.tensor_scalar(wchunk[:, 0:D], wchunk[:, 0:D], RND, None, ALU.subtract)
                V.tensor_scalar(wchunk[:, 0:D], wchunk[:, 0:D], 1.0, None, ALU.min)
                V.tensor_scalar(w2b[0][:, :], wchunk[:, 0:D], -1.0, None, ALU.max)
                ctx.wV(nc.sync)
                st.inc(nc.sync.dma_start(w2scr[c * 128:(c + 1) * 128, :], w2b[0][:, :]))
            nc.sync.wait_ge(st.h, st.n)
            ctx.wV(nc.tensor)
            for tc in range(4):
                t0 = tc * 512
                for fc in range(NFB):
                    pt = (p0, p1)[fc % 2]
                    ctx.wS(nc.tensor)
                    for b in range(DB):
                        ii = nc.tensor.matmul(pt[:, 0:512],
                                              w1t[:, b * FF + fc * 128: b * FF + fc * 128 + 128],
                                              xn[:, b * T + t0: b * T + t0 + 512],
                                              start=(b == 0), stop=(b == DB - 1))
                    sT.inc(ii, 1)
                    nc.scalar.wait_ge(sT.h, sT.n)
                    SC.activation(h[:, fc * 512:(fc + 1) * 512], pt[:, 0:512],
                                  AF.Gelu, bias=b1t[:, fc:fc + 1])
                ctx.wV(nc.sync)
                ld.inc(nc.sync.dma_start(sb3(xre, 512), x_chunk_ap(x_d, t0, 512, T)))
                for grp in range(2):
                    banks = (p2, p3, p4, p5)
                    for fb in range(NFB):
                        wb = w2b[fb % 2]
                        nc.sync.wait_ge(sT.h, sT.n)
                        ld.inc(nc.sync.dma_start(wb[:, :], w2scr[fb * 128:(fb + 1) * 128, :]))
                        nc.tensor.wait_ge(ld.h, ld.n)
                        if fb == 0:
                            ctx.wV(nc.tensor)
                        for dc in range(4):
                            di = grp * 4 + dc
                            ii = nc.tensor.matmul(banks[dc][:, 0:512],
                                                  wb[:, di * 128:(di + 1) * 128],
                                                  h[:, fb * 512:(fb + 1) * 512],
                                                  start=(fb == 0), stop=(fb == NFB - 1))
                        sT.inc(ii, 1)
                    nc.vector.wait_ge(sT.h, sT.n)
                    nc.vector.wait_ge(ld.h, ld.n)
                    for dc in range(4):
                        di = grp * 4 + dc
                        V.tensor_scalar(nxc[:, di * 512:(di + 1) * 512], banks[dc][:, 0:512],
                                        b2t[:, di:di + 1], None, ALU.add)
                        V.tensor_add(nxc[:, di * 512:(di + 1) * 512],
                                     nxc[:, di * 512:(di + 1) * 512],
                                     xre[:, di * 512:(di + 1) * 512])
                ctx.wV(nc.sync)
                st.inc(nc.sync.dma_start(x_chunk_ap(xo_d, t0, 512, T), sb3(nxc, 512)))
            nc.sync.wait_ge(st.h, st.n)
    return nc

def prep_l0f(full, b, r):
    xc = full["x_cur"]
    return {
        "x": np.ascontiguousarray(xc[b, r * T:(r + 1) * T, :].T),
        "w1T": np.ascontiguousarray(full["L0_ffn_W1"].T),
        "b1t": np.ascontiguousarray(full["L0_ffn_b1"].reshape(32, 128).T),
        "w2T": np.ascontiguousarray(full["L0_ffn_W2"].T),
        "b2t": np.ascontiguousarray(full["L0_ffn_b2"].reshape(8, 128).T),
    }

def build_l1kv():
    nc = bass.Bass("TRN2", target_bir_lowering=False, detect_race_conditions=False)
    stack = ExitStack()
    x_d = nc.dram_tensor("x", [D, T], F32, kind="ExternalInput")
    wqt_d = nc.dram_tensor("wqT", [D, D], F32, kind="ExternalInput")
    wkt_d = nc.dram_tensor("wkT", [D, D], F32, kind="ExternalInput")
    wvt_d = nc.dram_tensor("wvT", [D, D], F32, kind="ExternalInput")
    qf_d = nc.dram_tensor("qf", [D, T], BF16, kind="ExternalOutput")
    kf_d = nc.dram_tensor("kf", [D, T], BF16, kind="ExternalOutput")
    vt_d = nc.dram_tensor("vT", [T, D], BF16, kind="ExternalOutput")
    with stack:
        ctx = Ctx(nc, stack)
        V, SC = ctx.V, ctx.SC
        xre = ctx.sbuf("xre", [128, DB * 512])
        xn = ctx.sbuf("xn", [128, DB * T], BF16)
        wq = ctx.sbuf("wq", [128, DB * D], BF16)
        wk = ctx.sbuf("wk", [128, DB * D], BF16)
        wv = ctx.sbuf("wv", [128, DB * D], BF16)
        ob = ctx.sbuf("ob", [128, 2 * 512], BF16)
        onesc = ctx.sbuf("onesc", [128, 1])
        ones1 = ctx.sbuf("ones1", [1, 128])
        p0 = ctx.psum("p0", [128, 512]); p1 = ctx.psum("p1", [128, 512])
        ld, st, ldg = ctx.S("ld"), ctx.S("st"), ctx.S("ldg")
        sT = ctx.S("mT")
        with nc.Block():
            V.memset(onesc[:, :], 1.0)
            V.memset(ones1[:, :], 1.0)
            for wd, ws in ((wqt_d, wq), (wkt_d, wk), (wvt_d, wv)):
                ldg.inc(nc.gpsimd.dma_start(ws[:, :].rearrange("p (b g) -> p b g", b=DB),
                                            bass.AP(wd, 0, [[D, 128], [128 * D, DB], [1, D]])))
            for tc in range(4):
                t0 = tc * 512
                ctx.wV(nc.sync)
                ctx.wS(nc.sync)
                ld.inc(nc.sync.dma_start(sb3(xre, 512), x_chunk_ap(x_d, t0, 512, T)))
                for e in (nc.tensor, nc.vector, nc.scalar):
                    e.wait_ge(ld.h, ld.n)
                emit_ln(ctx, xre, 0, 512, xn, t0, T, 0, 512)
            nc.tensor.wait_ge(ldg.h, ldg.n)
            ctx.wV(nc.tensor)
            for wsb, od in ((wq, qf_d), (wk, kf_d)):
                for hc in range(DB):
                    for tc in range(4):
                        t0 = tc * 512
                        pt = (p0, p1)[(hc * 4 + tc) % 2]
                        ctx.wS(nc.tensor)
                        for b in range(DB):
                            ii = nc.tensor.matmul(pt[:, 0:512],
                                                  wsb[:, b * D + hc * 128: b * D + hc * 128 + 128],
                                                  xn[:, b * T + t0: b * T + t0 + 512],
                                                  start=(b == 0), stop=(b == DB - 1))
                        sT.inc(ii, 1)
                        nc.scalar.wait_ge(sT.h, sT.n)
                        pp = (hc * 4 + tc) % 2
                        nc.scalar.wait_ge(st.h, st.n)
                        SC.activation(ob[:, pp * 512:(pp + 1) * 512], pt[:, 0:512], AF.Identity)
                        ctx.wS(nc.sync)
                        st.inc(nc.sync.dma_start(od[hc * 128:(hc + 1) * 128, t0:t0 + 512],
                                                 ob[:, pp * 512:(pp + 1) * 512]))
            for tt in range(16):
                for hc in range(DB):
                    pt = (p0, p1)[(tt * DB + hc) % 2]
                    ctx.wS(nc.tensor)
                    for b in range(DB):
                        ii = nc.tensor.matmul(pt[:, 0:128],
                                              xn[:, b * T + tt * 128: b * T + tt * 128 + 128],
                                              wv[:, b * D + hc * 128: b * D + hc * 128 + 128],
                                              start=(b == 0), stop=(b == DB - 1))
                    sT.inc(ii, 1)
                    nc.scalar.wait_ge(sT.h, sT.n)
                    pp = (tt * DB + hc) % 2
                    nc.scalar.wait_ge(st.h, st.n)
                    SC.activation(ob[:, pp * 512: pp * 512 + 128], pt[:, 0:128], AF.Identity)
                    ctx.wS(nc.sync)
                    st.inc(nc.sync.dma_start(vt_d[tt * 128:(tt + 1) * 128, hc * 128:(hc + 1) * 128],
                                             ob[:, pp * 512: pp * 512 + 128]))
            nc.sync.wait_ge(st.h, st.n)
    return nc

def prep_l1kv(full, b, r):
    xc = full["x_cur"]
    WK = full["L1_attn_W_K"].reshape(1024, 1024)
    WQ = full["L1_attn_W_Q"].reshape(1024, 1024)
    WV = full["L1_attn_W_V"].reshape(1024, 1024)
    return {
        "x": np.ascontiguousarray(xc[b, r * T:(r + 1) * T, :].T),
        "wqT": np.ascontiguousarray(WQ.T), "wkT": np.ascontiguousarray(WK.T),
        "wvT": np.ascontiguousarray(WV.T),
    }

def build_l1main():
    nc = bass.Bass("TRN2", target_bir_lowering=False, detect_race_conditions=False)
    stack = ExitStack()
    x_d = nc.dram_tensor("x", [D, T], F32, kind="ExternalInput")
    qf_d = nc.dram_tensor("qf", [D, T], BF16, kind="ExternalInput")
    kf_d = nc.dram_tensor("kf_all", [D, 2 * T], BF16, kind="ExternalInput")
    vt_d = nc.dram_tensor("vT_all", [2 * T, D], BF16, kind="ExternalInput")
    mk_d = nc.dram_tensor("masks", [128, 12 * 512], BF16, kind="ExternalInput")
    wo_d = nc.dram_tensor("woT", [D, D], F32, kind="ExternalInput")
    u1_d = nc.dram_tensor("u1", [D, 256], F32, kind="ExternalInput")
    v1_d = nc.dram_tensor("v1", [256, 4096], F32, kind="ExternalInput")
    b1_d = nc.dram_tensor("b1t", [128, 32], F32, kind="ExternalInput")
    u2_d = nc.dram_tensor("u2", [4096, 256], F32, kind="ExternalInput")
    v2_d = nc.dram_tensor("v2", [256, D], F32, kind="ExternalInput")
    b2_d = nc.dram_tensor("b2t", [128, 8], F32, kind="ExternalInput")
    xo_d = nc.dram_tensor("xo", [D, T], F32, kind="ExternalOutput")
    FF = 4096
    NFB = FF // 128
    with stack:
        ctx = Ctx(nc, stack)
        V, SC = ctx.V, ctx.SC
        xs = ctx.sbuf("xs", [128, DB * T])
        qf = ctx.sbuf("qf_s", [128, DB * T], BF16)
        kg = ctx.sbuf("kg", [128, 2 * 2 * T], BF16)
        vg = ctx.sbuf("vg", [128, 32 * 256], BF16)
        zb = ctx.sbuf("zb", [128, DB * T], BF16)
        mks = ctx.sbuf("mks", [128, 12 * 512], BF16)
        pb = ctx.sbuf("pb", [128, 512], BF16)
        lnrow_pre = ctx.sbuf("lnrow", [1, 3 * 512])
        lnsq_pre = ctx.sbuf("lnsq", [128, 2 * 512])
        lnrb_pre = ctx.sbuf("lnrb", [128, 1024])
        rs = lnrow_pre
        wo = ctx.sbuf("wo", [128, DB * D], BF16)
        wu1 = ctx.sbuf("wu1", [128, DB * 256], BF16)
        wv1 = vg
        wu2 = kg
        b1t = ctx.sbuf("b1t_s", [128, 32])
        b2t = ctx.sbuf("b2t_s", [128, 8])
        Pq = mks
        onesc = ctx.sbuf("onesc", [128, 1])
        ones1 = ctx.sbuf("ones1", [1, 128])
        nbias = ctx.sbuf("nbias", [128, 1])
        onescb = ctx.sbuf("onescb", [128, 1], BF16)
        p0 = ctx.psum("p0", [128, 512]); p1 = ctx.psum("p1", [128, 512])
        p2 = ctx.psum("p2", [128, 512]); p3 = ctx.psum("p3", [128, 512])
        p4 = ctx.psum("p4", [128, 512]); p5 = ctx.psum("p5", [128, 512])
        ld, st, ldg = ctx.S("ld"), ctx.S("st"), ctx.S("ldg")
        sT = ctx.S("mT")
        with nc.Block():
            V.memset(nbias[:, :], -16.0)
            V.memset(onescb[:, :], 1.0)
            ld.inc(nc.sync.dma_start(sb3(xs, T), x_re(x_d)))
            ld.inc(nc.sync.dma_start(qf[:, :].rearrange("p (b t) -> p b t", b=DB),
                                     bass.AP(qf_d, 0, [[T, 128], [128 * T, DB], [1, T]])))
            ld.inc(nc.sync.dma_start(mks[:, :], mk_d[:, :]))
            ldg.inc(nc.gpsimd.dma_start(wo[:, :].rearrange("p (b g) -> p b g", b=DB),
                                        bass.AP(wo_d, 0, [[D, 128], [128 * D, DB], [1, D]])))
            ldg.inc(nc.gpsimd.dma_start(wu1[:, :].rearrange("p (b g) -> p b g", b=DB),
                                        bass.AP(u1_d, 0, [[256, 128], [128 * 256, DB], [1, 256]])))
            ld.inc(nc.sync.dma_start(b1t[:, :], b1_d[:, :]))
            ld.inc(nc.sync.dma_start(b2t[:, :], b2_d[:, :]))
            V.memset(onesc[:, :], 1.0)
            V.memset(ones1[:, :], 1.0)
            for e in (nc.tensor, nc.vector, nc.scalar):
                e.wait_ge(ld.h, ld.n)
            ctx.wV(nc.tensor)
            for hg in range(4):
                nc.sync.wait_ge(sT.h, sT.n)
                ld.inc(nc.sync.dma_start(kg[:, :].rearrange("p (b t) -> p b t", b=2),
                                         bass.AP(kf_d, hg * 256 * 2 * T, [[2 * T, 128], [128 * 2 * T, 2], [1, 2 * T]])))
                ld.inc(nc.sync.dma_start(vg[:, :].rearrange("p (b t) -> p b t", b=32),
                                         bass.AP(vt_d, hg * 256, [[D, 128], [128 * D, 32], [1, 256]])))
                for e in (nc.tensor, nc.scalar):
                    e.wait_ge(ld.h, ld.n)
                for hi in range(4):
                    hb = (hi % 2) * 64
                    kc = hi // 2
                    for cqc in range(4):
                        t0 = cqc * 512
                        jmax = 8 * cqc + 8
                        for j in range(jmax):
                            nc.tensor.wait_ge(ctx.SC.sem.h, ctx.SC.sem.n)
                            ii = nc.tensor.matmul(
                                p0[0:128, 0:512],
                                kg[:, kc * 2 * T + j * 128: kc * 2 * T + j * 128 + 128][hb:hb + 64, :],
                                qf[:, (hg * 2 + kc) * T + t0: (hg * 2 + kc) * T + t0 + 512][hb:hb + 64, :],
                                start=True, stop=True)
                            sT.inc(ii, 1)
                            nc.scalar.wait_ge(sT.h, sT.n)
                            ctx.wV(nc.scalar)
                            SC.activation(pb[:, :], p0[:, 0:512], AF.Exp, scale=0.125, bias=nbias[:, 0:1])
                            w = 8 * cqc - j + 7
                            if 0 <= w < 12:
                                ctx.wS(nc.vector)
                                V.tensor_mul(pb[:, :], pb[:, :], mks[:, w * 512:(w + 1) * 512])
                                ctx.wV(nc.tensor)
                            else:
                                ctx.wS(nc.tensor)
                            nc.tensor.matmul(p2[0:64, 0:512],
                                             vg[:, j * 256 + hi * 64: j * 256 + hi * 64 + 64],
                                             pb[:, :], start=(j == 0), stop=(j == jmax - 1))
                            ii = nc.tensor.matmul(p3[0:1, 0:512], onescb[:, 0:1], pb[:, :],
                                                  start=(j == 0), stop=(j == jmax - 1))
                            sT.inc(ii, 1)
                        nc.vector.wait_ge(sT.h, sT.n)
                        V.reciprocal(rs[0:1, 0:512], p3[0:1, 0:512])
                        ctx.wV(nc.tensor)
                        ii = nc.tensor.matmul(p4[0:64, 0:512], ones1[0:1, 0:64], rs[0:1, 0:512],
                                              start=True, stop=True)
                        sT.inc(ii, 1)
                        nc.vector.wait_ge(sT.h, sT.n)
                        V.tensor_copy(lnrb_pre[0:64, 0:512], p4[0:64, 0:512])
                        dst = zb[:, (hg * 2 + kc) * T + t0: (hg * 2 + kc) * T + t0 + 512][hb:hb + 64, :]
                        V.tensor_mul(dst, p2[0:64, 0:512], lnrb_pre[0:64, 0:512])
            ctx.wV(nc.tensor)
            for db in range(DB):
                for tc in range(4):
                    t0 = tc * 512
                    pt = (p0, p1)[(db * 4 + tc) % 2]
                    ctx.wV(nc.tensor)
                    for hc in range(DB):
                        ii = nc.tensor.matmul(pt[:, 0:512],
                                              wo[:, hc * D + db * 128: hc * D + db * 128 + 128],
                                              zb[:, hc * T + t0: hc * T + t0 + 512],
                                              start=(hc == 0), stop=(hc == DB - 1))
                    sT.inc(ii, 1)
                    nc.vector.wait_ge(sT.h, sT.n)
                    xv = xs[:, db * T + t0: db * T + t0 + 512]
                    V.tensor_add(xv, xv, pt[:, 0:512])
            nc.gpsimd.wait_ge(sT.h, sT.n)
            ldg.inc(nc.gpsimd.dma_start(wv1[:, :].rearrange("p (b g) -> p b g", b=2),
                                        bass.AP(v1_d, 0, [[4096, 128], [128 * 4096, 2], [1, 4096]])))
            ldg.inc(nc.gpsimd.dma_start(wu2[:, :].rearrange("p (b g) -> p b g", b=NFB),
                                        bass.AP(u2_d, 0, [[256, 128], [128 * 256, NFB], [1, 256]])))
            ldg.inc(nc.gpsimd.dma_start(mks[:, 4096:4096 + 2 * D].rearrange("p (b g) -> p b g", b=2),
                                        bass.AP(v2_d, 0, [[D, 128], [128 * D, 2], [1, D]])))
            emit_ln(ctx, xs, 0, T, qf, 0, T, 0, T)
            nc.tensor.wait_ge(ldg.h, ldg.n)
            for rc in range(2):
                for tc in range(4):
                    t0 = tc * 512
                    pt = (p2, p3)[(rc * 4 + tc) % 2]
                    ctx.wS(nc.tensor)
                    ctx.wV(nc.tensor)
                    for b in range(DB):
                        ii = nc.tensor.matmul(pt[:, 0:512],
                                              wu1[:, b * 256 + rc * 128: b * 256 + rc * 128 + 128],
                                              qf[:, b * T + t0: b * T + t0 + 512],
                                              start=(b == 0), stop=(b == DB - 1))
                    sT.inc(ii, 1)
                    nc.scalar.wait_ge(sT.h, sT.n)
                    SC.activation(Pq[:, rc * T + t0: rc * T + t0 + 512], pt[:, 0:512], AF.Identity)
            GA = 0.7978845608028654
            GB = 0.044715 * GA
            sq1 = lnsq_pre[:, 0:512]
            sq2 = pb
            for tc in range(4):
                t0 = tc * 512
                for fc in range(NFB):
                    pt = (p0, p1)[fc % 2]
                    ctx.wS(nc.tensor)
                    ctx.wV(nc.tensor)
                    for rc in range(2):
                        ii = nc.tensor.matmul(pt[:, 0:512],
                                              wv1[:, rc * 4096 + fc * 128: rc * 4096 + fc * 128 + 128],
                                              Pq[:, rc * T + t0: rc * T + t0 + 512],
                                              start=(rc == 0), stop=(rc == 1))
                    sT.inc(ii, 1)
                    nc.scalar.wait_ge(sT.h, sT.n)
                    SC.activation(sq1, pt[:, 0:512], AF.Identity, bias=b1t[:, fc:fc + 1])
                    ctx.wS(nc.scalar)
                    SC.activation(sq2[:, :], sq1, AF.Square)
                    ctx.wS(nc.vector)
                    V.tensor_mul(sq2[:, :], sq2[:, :], sq1)
                    V.scalar_tensor_tensor(sq2[:, :], sq2[:, :], GB / GA, sq1,
                                           ALU.mult, ALU.add)
                    ctx.wV(nc.scalar)
                    SC.activation(sq2[:, :], sq2[:, :], AF.Sigmoid, scale=2.0 * GA)
                    ctx.wS(nc.vector)
                    V.tensor_mul(zb[:, fc * 512: fc * 512 + 512], sq1, sq2[:, :])
                ctx.wV(nc.tensor)
                for rc in range(2):
                    pt = (p4, p5)[rc]
                    for fb in range(NFB):
                        ii = nc.tensor.matmul(pt[:, 0:512],
                                              wu2[:, fb * 256 + rc * 128: fb * 256 + rc * 128 + 128],
                                              zb[:, fb * 512: fb * 512 + 512],
                                              start=(fb == 0), stop=(fb == NFB - 1))
                    sT.inc(ii, 1)
                nc.scalar.wait_ge(sT.h, sT.n)
                SC.activation(Pq[:, t0:t0 + 512], p4[:, 0:512], AF.Identity)
                SC.activation(Pq[:, T + t0: T + t0 + 512], p5[:, 0:512], AF.Identity)
                ctx.wS(nc.tensor)
                for db in range(DB):
                    pt = (p2, p3)[db % 2]
                    ctx.wV(nc.tensor)
                    for rc in range(2):
                        ii = nc.tensor.matmul(pt[:, 0:512],
                                              mks[:, 4096 + rc * D + db * 128: 4096 + rc * D + db * 128 + 128],
                                              Pq[:, rc * T + t0: rc * T + t0 + 512],
                                              start=(rc == 0), stop=(rc == 1))
                    sT.inc(ii, 1)
                    nc.vector.wait_ge(sT.h, sT.n)
                    xv = xs[:, db * T + t0: db * T + t0 + 512]
                    V.tensor_scalar(xv, xv, b2t[:, db:db + 1], None, ALU.add)
                    V.tensor_add(xv, xv, pt[:, 0:512])
            ctx.wV(nc.sync)
            st.inc(nc.sync.dma_start(x_re(xo_d), sb3(xs, T)))
            nc.sync.wait_ge(st.h, st.n)
    return nc

def l1_tile_map(r):
    return [2 * s + r for s in range(16)]

def prep_l1main(full, b, r, qf_pair, kf_pair, vt_pair):
    xc = full["x_cur"]
    tiles = l1_tile_map(r)
    x = np.concatenate([xc[b, g * 128:(g + 1) * 128, :].T for g in tiles], axis=1)
    qf = np.concatenate([qf_pair[g // 16][:, (g % 16) * 128:(g % 16) * 128 + 128] for g in tiles], axis=1)
    kf_all = np.concatenate([kf_pair[0], kf_pair[1]], axis=1)
    vt_all = np.concatenate([vt_pair[0], vt_pair[1]], axis=0)
    masks = np.zeros((128, 12 * 512), np.float32)
    p = np.arange(128)[:, None]
    jj = np.arange(512)[None, :]
    for w in range(12):
        thr = 128 * (w - 7 + r) + 256 * (jj // 128) + (jj % 128)
        masks[:, w * 512:(w + 1) * 512] = (p <= thr).astype(np.float32)
    import ml_dtypes
    return {
        "x": np.ascontiguousarray(x.astype(np.float32)),
        "qf": np.ascontiguousarray(qf),
        "kf_all": np.ascontiguousarray(kf_all),
        "vT_all": np.ascontiguousarray(vt_all),
        "masks": masks.astype(ml_dtypes.bfloat16),
        "woT": np.ascontiguousarray(full["L1_attn_W_O"].T),
        "u1": np.ascontiguousarray(full["L1_ffn_U1"]),
        "v1": np.ascontiguousarray(full["L1_ffn_V1"]),
        "b1t": np.ascontiguousarray(full["L1_ffn_b1"].reshape(32, 128).T),
        "u2": np.ascontiguousarray(full["L1_ffn_U2"]),
        "v2": np.ascontiguousarray(full["L1_ffn_V2"]),
        "b2t": np.ascontiguousarray(full["L1_ffn_b2"].reshape(8, 128).T),
    }

def gather_l1(results, key="xo"):
    out = np.zeros((4, 4096, 1024), np.float32)
    for c in range(8):
        b, r = c // 2, c % 2
        xo = np.asarray(results[c][key])
        for s, g in enumerate(l1_tile_map(r)):
            out[b, g * 128:(g + 1) * 128, :] = xo[:, s * 128:(s + 1) * 128].T
    return out

def build_l2kv():
    nc = bass.Bass("TRN2", target_bir_lowering=False, detect_race_conditions=False)
    stack = ExitStack()
    x_d = nc.dram_tensor("x", [D, T], F32, kind="ExternalInput")
    wq_d = nc.dram_tensor("wq", [D, D], F32, kind="ExternalInput")
    wk_d = nc.dram_tensor("wk", [D, D], F32, kind="ExternalInput")
    wv_d = nc.dram_tensor("wv", [D, D], F32, kind="ExternalInput")
    g1_d = nc.dram_tensor("g1w", [D, 64], F32, kind="ExternalInput")
    th_d = nc.dram_tensor("thcol", [128, 16], F32, kind="ExternalInput")
    pos_d = nc.dram_tensor("posrow", [1, T], F32, kind="ExternalInput")
    qt_d = nc.dram_tensor("qt", [2 * D, T], BF16, kind="ExternalOutput")
    kt_d = nc.dram_tensor("kt", [2 * D, T], BF16, kind="ExternalOutput")
    vt_d = nc.dram_tensor("vT", [T, D], BF16, kind="ExternalOutput")
    g1o_d = nc.dram_tensor("g1o", [64, T], BF16, kind="ExternalOutput")
    TWOPI = 6.283185307179586
    with stack:
        ctx = Ctx(nc, stack)
        V, SC = ctx.V, ctx.SC
        xre = ctx.sbuf("xre", [128, DB * 512])
        xn = ctx.sbuf("xn", [128, DB * T], BF16)
        wqs = ctx.sbuf("wqs", [128, DB * D], BF16)
        wvs = ctx.sbuf("wvs", [128, DB * D], BF16)
        g1s = ctx.sbuf("g1s", [128, DB * 64], BF16)
        qs = ctx.sbuf("qs", [128, DB * T], BF16)
        sc = ctx.sbuf("scc", [128, 2 * T], BF16)
        ang = ctx.sbuf("ang", [128, 2 * T])
        rr = ctx.sbuf("rr", [128, T])
        rr2 = ctx.sbuf("rr2", [128, 2 * T], BF16)
        thc = ctx.sbuf("thc", [128, 16])
        posb = ctx.sbuf("posb", [128, T])
        posr = ctx.sbuf("posr", [1, T])
        ob = ctx.sbuf("ob", [128, 2 * 512], BF16)
        onesc = ctx.sbuf("onesc", [128, 1])
        ones1 = ctx.sbuf("ones1", [1, 128])
        p0 = ctx.psum("p0", [128, 512]); p1 = ctx.psum("p1", [128, 512])
        ld, st, ldg = ctx.S("ld"), ctx.S("st"), ctx.S("ldg")
        sT = ctx.S("mT")

        def silu_proj(wsb, dst):
            for hc in range(DB):
                for tc in range(4):
                    t0 = tc * 512
                    pt = (p0, p1)[(hc * 4 + tc) % 2]
                    ctx.wS(nc.tensor)
                    for b in range(DB):
                        ii = nc.tensor.matmul(pt[:, 0:512],
                                              wsb[:, b * D + hc * 128: b * D + hc * 128 + 128],
                                              xn[:, b * T + t0: b * T + t0 + 512],
                                              start=(b == 0), stop=(b == DB - 1))
                    sT.inc(ii, 1)
                    nc.scalar.wait_ge(sT.h, sT.n)
                    SC.activation(dst[:, hc * T + t0: hc * T + t0 + 512], pt[:, 0:512], AF.Silu)

        def lrpe_pass(src_buf, out_dram):
            for hi in range(16):
                hb = (hi % 2) * 64
                hc = hi // 2
                nc.vector.wait_ge(st.h, st.n)
                nc.scalar.wait_ge(st.h, st.n)
                V.tensor_scalar(ang[:, 0:T], posb[:, :], thc[:, hi:hi + 1], None, ALU.mult)
                V.tensor_scalar(ang[:, T:2 * T], ang[:, 0:T], 1.0, 1.5707963267948966, ALU.mult, ALU.add)
                for seg in range(2):
                    a = ang[:, seg * T:(seg + 1) * T]
                    V.tensor_scalar(rr[:, :], a, 1.0 / TWOPI, None, ALU.mult)
                    V.tensor_scalar(rr[:, :], rr[:, :], RND, None, ALU.add)
                    V.tensor_scalar(rr[:, :], rr[:, :], RND, None, ALU.subtract)
                    V.tensor_scalar(rr[:, :], rr[:, :], -TWOPI, None, ALU.mult)
                    V.tensor_add(a, a, rr[:, :])
                    V.tensor_scalar(a, a, 3.14159, None, ALU.min)
                    V.tensor_scalar(a, a, -3.14159, None, ALU.max)
                    ctx.wV(nc.scalar)
                    SC.activation(sc[:, T:2 * T] if seg == 0 else sc[:, 0:T], a, AF.Sin)
                ctx.wS(nc.vector)
                srch = src_buf[:, hc * T:(hc + 1) * T][hb:hb + 64, :]
                cosh = sc[hb:hb + 64, 0:T]
                sinh = sc[hb:hb + 64, T:2 * T]
                V.tensor_mul(rr2[hb:hb + 64, 0:T], srch, cosh)
                V.tensor_mul(rr2[hb:hb + 64, T:2 * T], srch, sinh)
                ctx.wV(nc.sync)
                st.inc(nc.sync.dma_start(out_dram[hi * 128: hi * 128 + 64, :], rr2[hb:hb + 64, 0:T]))
                st.inc(nc.sync.dma_start(out_dram[hi * 128 + 64: hi * 128 + 128, :], rr2[hb:hb + 64, T:2 * T]))

        with nc.Block():
            V.memset(onesc[:, :], 1.0)
            V.memset(ones1[:, :], 1.0)
            ld.inc(nc.sync.dma_start(thc[:, :], th_d[:, :]))
            ld.inc(nc.sync.dma_start(posr[:, :], pos_d[:, :]))
            ldg.inc(nc.gpsimd.dma_start(wqs[:, :].rearrange("p (b g) -> p b g", b=DB),
                                        bass.AP(wq_d, 0, [[D, 128], [128 * D, DB], [1, D]])))
            ldg.inc(nc.gpsimd.dma_start(g1s[:, :].rearrange("p (b g) -> p b g", b=DB),
                                        bass.AP(g1_d, 0, [[64, 128], [128 * 64, DB], [1, 64]])))
            for tc in range(4):
                t0 = tc * 512
                ctx.wV(nc.sync)
                ctx.wS(nc.sync)
                ld.inc(nc.sync.dma_start(sb3(xre, 512), x_chunk_ap(x_d, t0, 512, T)))
                for e in (nc.tensor, nc.vector, nc.scalar):
                    e.wait_ge(ld.h, ld.n)
                emit_ln(ctx, xre, 0, 512, xn, t0, T, 0, 512)
            nc.tensor.wait_ge(ld.h, ld.n)
            nc.tensor.wait_ge(ldg.h, ldg.n)
            for tc in range(4):
                ctx.wV(nc.tensor)
                ii = nc.tensor.matmul(p0[:, 0:512], ones1[0:1, :], posr[0:1, tc * 512:(tc + 1) * 512],
                                      start=True, stop=True)
                sT.inc(ii, 1)
                nc.vector.wait_ge(sT.h, sT.n)
                V.tensor_copy(posb[:, tc * 512:(tc + 1) * 512], p0[:, 0:512])
            for tc in range(4):
                t0 = tc * 512
                pt = (p0, p1)[tc % 2]
                ctx.wS(nc.tensor)
                ctx.wV(nc.tensor)
                for b in range(DB):
                    ii = nc.tensor.matmul(pt[0:64, 0:512], g1s[:, b * 64:(b + 1) * 64],
                                          xn[:, b * T + t0: b * T + t0 + 512],
                                          start=(b == 0), stop=(b == DB - 1))
                sT.inc(ii, 1)
                nc.scalar.wait_ge(sT.h, sT.n)
                nc.scalar.wait_ge(st.h, st.n)
                SC.activation(ob[0:64, 0:512], pt[0:64, 0:512], AF.Identity)
                ctx.wS(nc.sync)
                st.inc(nc.sync.dma_start(g1o_d[:, t0:t0 + 512], ob[0:64, 0:512]))
            silu_proj(wqs, qs)
            lrpe_pass(qs, qt_d)
            nc.gpsimd.wait_ge(sT.h, sT.n)
            nc.gpsimd.wait_ge(st.h, st.n)
            ldg.inc(nc.gpsimd.dma_start(wqs[:, :].rearrange("p (b g) -> p b g", b=DB),
                                        bass.AP(wk_d, 0, [[D, 128], [128 * D, DB], [1, D]])))
            nc.tensor.wait_ge(ldg.h, ldg.n)
            ctx.wV(nc.tensor)
            ctx.wS(nc.tensor)
            silu_proj(wqs, qs)
            lrpe_pass(qs, kt_d)
            nc.gpsimd.wait_ge(sT.h, sT.n)
            nc.gpsimd.wait_ge(st.h, st.n)
            ldg.inc(nc.gpsimd.dma_start(wvs[:, :].rearrange("p (b g) -> p b g", b=DB),
                                        bass.AP(wv_d, 0, [[D, 128], [128 * D, DB], [1, D]])))
            nc.tensor.wait_ge(ldg.h, ldg.n)
            for tt in range(16):
                for hc in range(DB):
                    pt = (p0, p1)[(tt * DB + hc) % 2]
                    ctx.wS(nc.tensor)
                    for b in range(DB):
                        ii = nc.tensor.matmul(pt[:, 0:128],
                                              xn[:, b * T + tt * 128: b * T + tt * 128 + 128],
                                              wvs[:, b * D + hc * 128: b * D + hc * 128 + 128],
                                              start=(b == 0), stop=(b == DB - 1))
                    sT.inc(ii, 1)
                    nc.scalar.wait_ge(sT.h, sT.n)
                    pp = (tt * DB + hc) % 2
                    nc.scalar.wait_ge(st.h, st.n)
                    SC.activation(ob[:, pp * 512: pp * 512 + 128], pt[:, 0:128], AF.Identity)
                    ctx.wS(nc.sync)
                    st.inc(nc.sync.dma_start(vt_d[tt * 128:(tt + 1) * 128, hc * 128:(hc + 1) * 128],
                                             ob[:, pp * 512: pp * 512 + 128]))
            nc.sync.wait_ge(st.h, st.n)
    return nc

def prep_l2kv(full, b, r):
    xc = full["x_cur"]
    th = np.asarray(full["L2_attn_theta"], np.float32).reshape(16, 64)
    thcol = np.zeros((128, 16), np.float32)
    thcol[0:64, :] = th.T
    thcol[64:128, :] = th.T
    pos = (np.arange(T, dtype=np.float32) + r * T)[None, :]
    return {
        "x": np.ascontiguousarray(xc[b, r * T:(r + 1) * T, :].T),
        "wq": np.ascontiguousarray(full["L2_attn_Wq"]),
        "wk": np.ascontiguousarray(full["L2_attn_Wk"]),
        "wv": np.ascontiguousarray(full["L2_attn_Wv"]),
        "g1w": np.ascontiguousarray(full["L2_attn_G1"]),
        "thcol": thcol, "posrow": pos,
    }

def build_l2main():
    nc = bass.Bass("TRN2", target_bir_lowering=False, detect_race_conditions=False)
    stack = ExitStack()
    x_d = nc.dram_tensor("x", [D, T], F32, kind="ExternalInput")
    qt_d = nc.dram_tensor("qt", [2 * D, T], BF16, kind="ExternalInput")
    kt_d = nc.dram_tensor("kt_all", [2 * D, 2 * T], BF16, kind="ExternalInput")
    vt_d = nc.dram_tensor("vT_all", [2 * T, D], BF16, kind="ExternalInput")
    g1_d = nc.dram_tensor("g1o", [64, T], BF16, kind="ExternalInput")
    mk_d = nc.dram_tensor("masks", [128, 12 * 512], BF16, kind="ExternalInput")
    wo_d = nc.dram_tensor("woS", [D, D], F32, kind="ExternalInput")
    g2_d = nc.dram_tensor("g2w", [64, D], F32, kind="ExternalInput")
    gm_d = nc.dram_tensor("gmt", [128, 8], F32, kind="ExternalInput")
    bt_d = nc.dram_tensor("btt", [128, 8], F32, kind="ExternalInput")
    xo_d = nc.dram_tensor("xo", [D, T], F32, kind="ExternalOutput")
    with stack:
        ctx = Ctx(nc, stack)
        V, SC = ctx.V, ctx.SC
        xs = ctx.sbuf("xs", [128, DB * T])
        qtg = ctx.sbuf("qtg", [128, 2 * T], BF16)
        ktg = ctx.sbuf("ktg", [128, 2 * 2 * T], BF16)
        vg = ctx.sbuf("vg", [128, 32 * 128], BF16)
        zb = ctx.sbuf("zb", [128, DB * T], BF16)
        mks = ctx.sbuf("mks", [128, 12 * 512], BF16)
        eb = ctx.sbuf("eb", [128, 512], BF16)
        wos = ctx.sbuf("wos", [128, DB * D], BF16)
        g2s = ctx.sbuf("g2s", [64, D], BF16)
        g1s = ctx.sbuf("g1s", [64, T], BF16)
        gms = ctx.sbuf("gms", [128, 8])
        bts = ctx.sbuf("bts", [128, 8])
        og = ctx.sbuf("og", [128, DB * 512])
        ogn = ctx.sbuf("ogn", [128, DB * 512], BF16)
        gbuf = ctx.sbuf("gbuf", [128, 512], BF16)
        onesc = ctx.sbuf("onesc", [128, 1])
        ones1 = ctx.sbuf("ones1", [1, 128])
        p0 = ctx.psum("p0", [128, 512]); p1 = ctx.psum("p1", [128, 512])
        p2 = ctx.psum("p2", [128, 512]); p3 = ctx.psum("p3", [128, 512])
        ld, st, ldg = ctx.S("ld"), ctx.S("st"), ctx.S("ldg")
        sT = ctx.S("mT")
        with nc.Block():
            V.memset(onesc[:, :], 1.0)
            V.memset(ones1[:, :], 1.0)
            ld.inc(nc.sync.dma_start(sb3(xs, T), x_re(x_d)))
            ld.inc(nc.sync.dma_start(mks[:, :], mk_d[:, :]))
            ld.inc(nc.sync.dma_start(g1s[:, :], g1_d[:, :]))
            ldg.inc(nc.gpsimd.dma_start(wos[:, :].rearrange("p (b g) -> p b g", b=DB),
                                        bass.AP(wo_d, 0, [[D, 128], [128 * D, DB], [1, D]])))
            ldg.inc(nc.gpsimd.dma_start(g2s[:, :], g2_d[:, :]))
            ld.inc(nc.sync.dma_start(gms[:, :], gm_d[:, :]))
            ld.inc(nc.sync.dma_start(bts[:, :], bt_d[:, :]))
            for e in (nc.tensor, nc.vector, nc.scalar):
                e.wait_ge(ld.h, ld.n)
                e.wait_ge(ldg.h, ldg.n)
            ctx.wV(nc.tensor)
            for hg in range(8):
                nc.sync.wait_ge(sT.h, sT.n)
                ld.inc(nc.sync.dma_start(qtg[:, :].rearrange("p (b t) -> p b t", b=2),
                                         bass.AP(qt_d, hg * 256 * T, [[T, 128], [128 * T, 2], [1, T]])))
                ld.inc(nc.sync.dma_start(ktg[:, :].rearrange("p (b t) -> p b t", b=2),
                                         bass.AP(kt_d, hg * 256 * 2 * T, [[2 * T, 128], [128 * 2 * T, 2], [1, 2 * T]])))
                ld.inc(nc.sync.dma_start(vg[:, :].rearrange("p (b t) -> p b t", b=32),
                                         bass.AP(vt_d, hg * 128, [[D, 128], [128 * D, 32], [1, 128]])))
                for e in (nc.tensor, nc.scalar):
                    e.wait_ge(ld.h, ld.n)
                for hi in range(2):
                    hb = hi * 64
                    for cqc in range(4):
                        t0 = cqc * 512
                        jmax = 8 * cqc + 8
                        for j in range(jmax):
                            nc.tensor.wait_ge(ctx.SC.sem.h, ctx.SC.sem.n)
                            ii = nc.tensor.matmul(
                                p0[0:128, 0:512],
                                ktg[:, hi * 2 * T + j * 128: hi * 2 * T + j * 128 + 128],
                                qtg[:, hi * T + t0: hi * T + t0 + 512],
                                start=True, stop=True)
                            sT.inc(ii, 1)
                            nc.scalar.wait_ge(sT.h, sT.n)
                            ctx.wV(nc.scalar)
                            SC.activation(eb[:, :], p0[:, 0:512], AF.Identity)
                            w = 8 * cqc - j + 7
                            if 0 <= w < 12:
                                ctx.wS(nc.vector)
                                V.tensor_mul(eb[:, :], eb[:, :], mks[:, w * 512:(w + 1) * 512])
                                ctx.wV(nc.tensor)
                            else:
                                ctx.wS(nc.tensor)
                            ii = nc.tensor.matmul(p2[0:64, 0:512],
                                                  vg[:, j * 128 + hb: j * 128 + hb + 64],
                                                  eb[:, :], start=(j == 0), stop=(j == jmax - 1))
                            sT.inc(ii, 1)
                        nc.vector.wait_ge(sT.h, sT.n)
                        dst = zb[:, hg * T + t0: hg * T + t0 + 512][hb:hb + 64, :]
                        V.tensor_copy(dst, p2[0:64, 0:512])
                        ctx.wV(nc.tensor)
            for tc in range(4):
                t0 = tc * 512
                for dc in range(DB):
                    pt = (p0, p1)[dc % 2]
                    pg = (p2, p3)[dc % 2]
                    ctx.wS(nc.tensor)
                    ctx.wV(nc.tensor)
                    for hc in range(DB):
                        nc.tensor.matmul(pt[:, 0:512],
                                         wos[:, hc * D + dc * 128: hc * D + dc * 128 + 128],
                                         zb[:, hc * T + t0: hc * T + t0 + 512],
                                         start=(hc == 0), stop=(hc == DB - 1))
                    ii = nc.tensor.matmul(pg[:, 0:512], g2s[:, dc * 128:(dc + 1) * 128],
                                          g1s[:, t0:t0 + 512], start=True, stop=True)
                    sT.inc(ii, 1)
                    nc.scalar.wait_ge(sT.h, sT.n)
                    SC.activation(gbuf[:, :], pg[:, 0:512], AF.Sigmoid)
                    ctx.wS(nc.vector)
                    nc.vector.wait_ge(sT.h, sT.n)
                    V.tensor_mul(og[:, dc * 512:(dc + 1) * 512], pt[:, 0:512], gbuf[:, :])
                emit_ln(ctx, og, 0, 512, ogn, 0, 512, 0, 512, gamma=gms, beta=bts)
                for dc in range(DB):
                    xv = xs[:, dc * T + t0: dc * T + t0 + 512]
                    V.tensor_add(xv, xv, ogn[:, dc * 512:(dc + 1) * 512])
            ctx.wV(nc.sync)
            st.inc(nc.sync.dma_start(x_re(xo_d), sb3(xs, T)))
            nc.sync.wait_ge(st.h, st.n)
    return nc

def prep_l2main(full, b, r, qt_pair, kt_pair, vt_pair, g1_pair):
    xc = full["x_cur"]
    tiles = l1_tile_map(r)
    x = np.concatenate([xc[b, g * 128:(g + 1) * 128, :].T for g in tiles], axis=1)
    qt = np.concatenate([qt_pair[g // 16][:, (g % 16) * 128:(g % 16) * 128 + 128] for g in tiles], axis=1)
    g1 = np.concatenate([g1_pair[g // 16][:, (g % 16) * 128:(g % 16) * 128 + 128] for g in tiles], axis=1)
    kt_all = np.concatenate([kt_pair[0], kt_pair[1]], axis=1)
    vt_all = np.concatenate([vt_pair[0], vt_pair[1]], axis=0)
    masks = np.zeros((128, 12 * 512), np.float32)
    p = np.arange(128)[:, None]
    jj = np.arange(512)[None, :]
    for w in range(12):
        thr = 128 * (w - 7 + r) + 256 * (jj // 128) + (jj % 128)
        masks[:, w * 512:(w + 1) * 512] = (p <= thr).astype(np.float32)
    import ml_dtypes
    return {
        "x": np.ascontiguousarray(x.astype(np.float32)),
        "qt": np.ascontiguousarray(qt),
        "kt_all": np.ascontiguousarray(kt_all),
        "vT_all": np.ascontiguousarray(vt_all),
        "g1o": np.ascontiguousarray(g1),
        "masks": masks.astype(ml_dtypes.bfloat16),
        "woS": np.ascontiguousarray(full["L2_attn_Wo"]),
        "g2w": np.ascontiguousarray(full["L2_attn_G2"]),
        "gmt": np.ascontiguousarray(full["L2_attn_gamma"].reshape(8, 128).T),
        "btt": np.ascontiguousarray(full["L2_attn_beta"].reshape(8, 128).T),
    }


def build_l2f():
    nc = bass.Bass("TRN2", target_bir_lowering=False, detect_race_conditions=False)
    stack = ExitStack()
    x_d = nc.dram_tensor("x", [D, T], F32, kind="ExternalInput")
    w1_d = nc.dram_tensor("w1t", [D, 4096], F32, kind="ExternalInput")
    w2_d = nc.dram_tensor("w2t", [D, 4096], F32, kind="ExternalInput")
    h1_d = nc.dram_tensor("h1t", [4096, 4096], F32, kind="ExternalInput")
    h2_d = nc.dram_tensor("h2t", [4096, 4096], F32, kind="ExternalInput")
    w3_d = nc.dram_tensor("w3t", [4096, D], F32, kind="ExternalInput")
    s1_d = nc.dram_tensor("s1t", [128, 32], F32, kind="ExternalInput")
    s2_d = nc.dram_tensor("s2t", [128, 32], F32, kind="ExternalInput")
    sh1_d = nc.dram_tensor("sh1t", [128, 32], F32, kind="ExternalInput")
    sh2_d = nc.dram_tensor("sh2t", [128, 32], F32, kind="ExternalInput")
    s3_d = nc.dram_tensor("s3t", [128, 8], F32, kind="ExternalInput")
    xo_d = nc.dram_tensor("xo", [D, T], F32, kind="ExternalOutput")
    FF = 4096
    NFB = FF // 128
    TC = 1024
    with stack:
        ctx = Ctx(nc, stack)
        V, SC = ctx.V, ctx.SC
        xnc = ctx.sbuf("xnc", [128, DB * TC], BF16)
        xrc = ctx.sbuf("xrc", [128, DB * TC])
        ha = ctx.sbuf("ha", [128, NFB * TC], BF16)
        hbb = ctx.sbuf("hbb", [128, NFB * TC], BF16)
        wA = [ctx.sbuf("wA0", [128, DB * 128], BF16)]
        wB = [ctx.sbuf("wB0", [128, DB * 128], BF16)]
        wH = [ctx.sbuf("wH0", [128, NFB * 128], BF16)]
        t1 = ctx.sbuf("t1b", [128, 512], BF16)
        t2 = ctx.sbuf("t2b", [128, 512], BF16)
        s1t = ctx.sbuf("s1t_s", [128, 32]); s2t = ctx.sbuf("s2t_s", [128, 32])
        sh1t = ctx.sbuf("sh1t_s", [128, 32]); sh2t = ctx.sbuf("sh2t_s", [128, 32])
        s3t = ctx.sbuf("s3t_s", [128, 8])
        onesc = ctx.sbuf("onesc", [128, 1])
        ones1 = ctx.sbuf("ones1", [1, 128])
        p0 = ctx.psum("p0", [128, 512]); p1 = ctx.psum("p1", [128, 512])
        ld, st, ldg = ctx.S("ld"), ctx.S("st"), ctx.S("ldg")
        sT = ctx.S("mT")

        def wslc(wdram, col0, nrow_blocks):
            return bass.AP(wdram, col0, [[wdram.shape[1], 128],
                                         [128 * wdram.shape[1], nrow_blocks], [1, 128]])

        with nc.Block():
            V.memset(onesc[:, :], 1.0)
            V.memset(ones1[:, :], 1.0)
            for sd, ss in ((s1_d, s1t), (s2_d, s2t), (sh1_d, sh1t), (sh2_d, sh2t), (s3_d, s3t)):
                ld.inc(nc.sync.dma_start(ss[:, :], sd[:, :]))
            for e in (nc.tensor, nc.vector, nc.scalar):
                e.wait_ge(ld.h, ld.n)
            for tci in range(T // TC):
                tb = tci * TC
                for sub in range(TC // 512):
                    t0 = tb + sub * 512
                    ctx.wV(nc.sync)
                    ctx.wS(nc.sync)
                    nc.sync.wait_ge(sT.h, sT.n)
                    ld.inc(nc.sync.dma_start(xrc[:, 0:DB * 512].rearrange("p (b t) -> p b t", b=DB),
                                             x_chunk_ap(x_d, t0, 512, T)))
                    for e in (nc.tensor, nc.vector, nc.scalar):
                        e.wait_ge(ld.h, ld.n)
                    emit_ln(ctx, xrc, 0, 512, xnc, sub * 512, TC, 0, 512)
                for fc in range(NFB):
                    nc.gpsimd.wait_ge(sT.h, sT.n)
                    ldg.inc(nc.gpsimd.dma_start(wA[0][:, :].rearrange("p (b t) -> p b t", b=DB),
                                                wslc(w1_d, fc * 128, DB)))
                    ldg.inc(nc.gpsimd.dma_start(wB[0][:, :].rearrange("p (b t) -> p b t", b=DB),
                                                wslc(w2_d, fc * 128, DB)))
                    nc.tensor.wait_ge(ldg.h, ldg.n)
                    for sub in range(2):
                        s0 = sub * 512
                        ctx.wS(nc.tensor)
                        ctx.wV(nc.tensor)
                        for b in range(DB):
                            nc.tensor.matmul(p0[:, 0:512], wA[0][:, b * 128:(b + 1) * 128],
                                             xnc[:, b * TC + s0: b * TC + s0 + 512],
                                             start=(b == 0), stop=(b == DB - 1))
                        for b in range(DB):
                            ii = nc.tensor.matmul(p1[:, 0:512], wB[0][:, b * 128:(b + 1) * 128],
                                                  xnc[:, b * TC + s0: b * TC + s0 + 512],
                                                  start=(b == 0), stop=(b == DB - 1))
                        sT.inc(ii, 1)
                        nc.scalar.wait_ge(sT.h, sT.n)
                        SC.activation(t1[:, :], p0[:, 0:512], AF.Silu, scale=s1t[:, fc:fc + 1])
                        SC.activation(t2[:, :], p1[:, 0:512], AF.Identity, scale=s2t[:, fc:fc + 1])
                        ctx.wS(nc.vector)
                        V.tensor_mul(ha[:, fc * TC + s0: fc * TC + s0 + 512], t1[:, :], t2[:, :])
                for fc in range(NFB):
                    nc.gpsimd.wait_ge(sT.h, sT.n)
                    ldg.inc(nc.gpsimd.dma_start(wH[0][:, :].rearrange("p (b t) -> p b t", b=NFB),
                                                wslc(h1_d, fc * 128, NFB)))
                    nc.tensor.wait_ge(ldg.h, ldg.n)
                    for sub in range(2):
                        s0 = sub * 512
                        ctx.wS(nc.tensor)
                        for b in range(NFB):
                            ii = nc.tensor.matmul(p0[:, 0:512], wH[0][:, b * 128:(b + 1) * 128],
                                                  ha[:, b * TC + s0: b * TC + s0 + 512],
                                                  start=(b == 0), stop=(b == NFB - 1))
                        sT.inc(ii, 1)
                        nc.scalar.wait_ge(sT.h, sT.n)
                        SC.activation(hbb[:, fc * TC + s0: fc * TC + s0 + 512], p0[:, 0:512],
                                      AF.Gelu, scale=sh1t[:, fc:fc + 1])
                for fc in range(NFB):
                    nc.gpsimd.wait_ge(sT.h, sT.n)
                    ldg.inc(nc.gpsimd.dma_start(wH[0][:, :].rearrange("p (b t) -> p b t", b=NFB),
                                                wslc(h2_d, fc * 128, NFB)))
                    nc.tensor.wait_ge(ldg.h, ldg.n)
                    for sub in range(2):
                        s0 = sub * 512
                        ctx.wS(nc.tensor)
                        for b in range(NFB):
                            ii = nc.tensor.matmul(p0[:, 0:512], wH[0][:, b * 128:(b + 1) * 128],
                                                  hbb[:, b * TC + s0: b * TC + s0 + 512],
                                                  start=(b == 0), stop=(b == NFB - 1))
                        sT.inc(ii, 1)
                        nc.scalar.wait_ge(sT.h, sT.n)
                        SC.activation(ha[:, fc * TC + s0: fc * TC + s0 + 512], p0[:, 0:512],
                                      AF.Gelu, scale=sh2t[:, fc:fc + 1])
                ctx.wV(nc.sync)
                nc.sync.wait_ge(sT.h, sT.n)
                ld.inc(nc.sync.dma_start(xrc[:, :].rearrange("p (b t) -> p b t", b=DB),
                                         x_chunk_ap(x_d, tb, TC, T)))
                for dc in range(DB):
                    nc.gpsimd.wait_ge(sT.h, sT.n)
                    ldg.inc(nc.gpsimd.dma_start(wH[0][:, :].rearrange("p (b t) -> p b t", b=NFB),
                                                wslc(w3_d, dc * 128, NFB)))
                    nc.tensor.wait_ge(ldg.h, ldg.n)
                    nc.vector.wait_ge(ld.h, ld.n)
                    for sub in range(2):
                        s0 = sub * 512
                        ctx.wV(nc.tensor)
                        for b in range(NFB):
                            ii = nc.tensor.matmul(p0[:, 0:512], wH[0][:, b * 128:(b + 1) * 128],
                                                  ha[:, b * TC + s0: b * TC + s0 + 512],
                                                  start=(b == 0), stop=(b == NFB - 1))
                        sT.inc(ii, 1)
                        nc.vector.wait_ge(sT.h, sT.n)
                        V.scalar_tensor_tensor(xrc[:, dc * TC + s0: dc * TC + s0 + 512],
                                               p0[:, 0:512], s3t[:, dc:dc + 1],
                                               xrc[:, dc * TC + s0: dc * TC + s0 + 512],
                                               ALU.mult, ALU.add)
                ctx.wV(nc.sync)
                st.inc(nc.sync.dma_start(x_chunk_ap(xo_d, tb, TC, T),
                                         xrc[:, :].rearrange("p (b t) -> p b t", b=DB)))
            nc.sync.wait_ge(st.h, st.n)
    return nc

def prep_l2f(full, b, r):
    xc = full["x_cur"]
    return {
        "x": np.ascontiguousarray(xc[b, r * T:(r + 1) * T, :].T),
        "w1t": np.ascontiguousarray(full["L2_ffn_w1_w"].T),
        "w2t": np.ascontiguousarray(full["L2_ffn_w2_w"].T),
        "h1t": np.ascontiguousarray(full["L2_ffn_h1_w"].T),
        "h2t": np.ascontiguousarray(full["L2_ffn_h2_w"].T),
        "w3t": np.ascontiguousarray(full["L2_ffn_w3_w"].T),
        "s1t": np.ascontiguousarray(full["L2_ffn_w1_s"].reshape(32, 128).T),
        "s2t": np.ascontiguousarray(full["L2_ffn_w2_s"].reshape(32, 128).T),
        "sh1t": np.ascontiguousarray(full["L2_ffn_h1_s"].reshape(32, 128).T),
        "sh2t": np.ascontiguousarray(full["L2_ffn_h2_s"].reshape(32, 128).T),
        "s3t": np.ascontiguousarray(full["L2_ffn_w3_s"].reshape(8, 128).T),
    }

def build_l3pre():
    nc = bass.Bass("TRN2", target_bir_lowering=False, detect_race_conditions=False)
    stack = ExitStack()
    x_d = nc.dram_tensor("x", [D, T], F32, kind="ExternalInput")
    win_d = nc.dram_tensor("winT", [D, 2 * D], F32, kind="ExternalInput")
    lnw_d = nc.dram_tensor("lnw", [128, 8], F32, kind="ExternalInput")
    lnb_d = nc.dram_tensor("lnb", [128, 8], F32, kind="ExternalInput")
    bu_d = nc.dram_tensor("binu", [128, 8], F32, kind="ExternalInput")
    bv_d = nc.dram_tensor("binv", [128, 8], F32, kind="ExternalInput")
    bh_d = nc.dram_tensor("bh", [128, 8], F32, kind="ExternalInput")
    xn1_d = nc.dram_tensor("xn1", [D, T], BF16, kind="ExternalOutput")
    g_d = nc.dram_tensor("g", [D, T], BF16, kind="ExternalOutput")
    vb_d = nc.dram_tensor("vb", [D, T], BF16, kind="ExternalOutput")
    with stack:
        ctx = Ctx(nc, stack)
        V, SC = ctx.V, ctx.SC
        xre = ctx.sbuf("xre", [128, DB * 512])
        xn1 = ctx.sbuf("xn1b", [128, DB * T], BF16)
        xn2 = ctx.sbuf("xn2b", [128, DB * T], BF16)
        win = ctx.sbuf("win", [128, DB * 2 * D], BF16)
        lnw = ctx.sbuf("lnw_s", [128, 8]); lnb = ctx.sbuf("lnb_s", [128, 8])
        bu = ctx.sbuf("bu_s", [128, 8]); bv = ctx.sbuf("bv_s", [128, 8])
        bh = ctx.sbuf("bh_s", [128, 8])
        bsum = ctx.sbuf("bsum", [128, 8])
        ob = ctx.sbuf("ob", [128, 2 * 512], BF16)
        onesc = ctx.sbuf("onesc", [128, 1])
        onescb = ctx.sbuf("onescb", [128, 1], BF16)
        ones1 = ctx.sbuf("ones1", [1, 128])
        p0 = ctx.psum("p0", [128, 512]); p1 = ctx.psum("p1", [128, 512])
        ld, st, ldg = ctx.S("ld"), ctx.S("st"), ctx.S("ldg")
        sT = ctx.S("mT")
        with nc.Block():
            V.memset(onesc[:, :], 1.0)
            V.memset(onescb[:, :], 1.0)
            V.memset(ones1[:, :], 1.0)
            for sd, ss in ((lnw_d, lnw), (lnb_d, lnb), (bu_d, bu), (bv_d, bv), (bh_d, bh)):
                ld.inc(nc.sync.dma_start(ss[:, :], sd[:, :]))
            ldg.inc(nc.gpsimd.dma_start(win[:, :].rearrange("p (b g) -> p b g", b=DB),
                                        bass.AP(win_d, 0, [[2 * D, 128], [128 * 2 * D, DB], [1, 2 * D]])))
            nc.vector.wait_ge(ld.h, ld.n)
            V.tensor_add(bsum[:, :], bv[:, :], bh[:, :])
            for tc in range(4):
                t0 = tc * 512
                ctx.wV(nc.sync)
                ctx.wS(nc.sync)
                ld.inc(nc.sync.dma_start(xre[:, :].rearrange("p (b t) -> p b t", b=DB),
                                         x_chunk_ap(x_d, t0, 512, T)))
                for e in (nc.tensor, nc.vector, nc.scalar):
                    e.wait_ge(ld.h, ld.n)
                emit_ln(ctx, xre, 0, 512, xn1, t0, T, 0, 512)
            emit_ln(ctx, xn1, 0, T, xn2, 0, T, 0, T, gamma=lnw, beta=lnb, src_bf16=True)
            ctx.wV(nc.sync)
            st.inc(nc.sync.dma_start(bass.AP(xn1_d, 0, [[T, 128], [128 * T, DB], [1, T]]),
                                     sb3(xn1, T)))
            for e in (nc.tensor,):
                e.wait_ge(ldg.h, ldg.n)
            ctx.wV(nc.tensor)
            for oc in range(16):
                for tc in range(4):
                    t0 = tc * 512
                    pt = (p0, p1)[(oc * 4 + tc) % 2]
                    ctx.wS(nc.tensor)
                    for b in range(DB):
                        ii = nc.tensor.matmul(pt[:, 0:512],
                                              win[:, b * 2 * D + oc * 128: b * 2 * D + oc * 128 + 128],
                                              xn2[:, b * T + t0: b * T + t0 + 512],
                                              start=(b == 0), stop=(b == DB - 1))
                    sT.inc(ii, 1)
                    nc.scalar.wait_ge(sT.h, sT.n)
                    pp = (oc * 4 + tc) % 2
                    nc.scalar.wait_ge(st.h, st.n)
                    if oc < 8:
                        SC.activation(ob[:, pp * 512:(pp + 1) * 512], pt[:, 0:512],
                                      AF.Sigmoid, bias=bu[:, oc:oc + 1])
                        od, row = g_d, oc
                    else:
                        SC.activation(ob[:, pp * 512:(pp + 1) * 512], pt[:, 0:512],
                                      AF.Identity, bias=bsum[:, oc - 8:oc - 7])
                        od, row = vb_d, oc - 8
                    ctx.wS(nc.sync)
                    st.inc(nc.sync.dma_start(od[row * 128:(row + 1) * 128, t0:t0 + 512],
                                             ob[:, pp * 512:(pp + 1) * 512]))
            nc.sync.wait_ge(st.h, st.n)
    return nc

def prep_l3pre(full, b, r):
    xc = full["x_cur"]
    return {
        "x": np.ascontiguousarray(xc[b, r * T:(r + 1) * T, :].T),
        "winT": np.ascontiguousarray(full["L3_attn_W_in"].T),
        "lnw": np.ascontiguousarray(full["L3_attn_ln_w"].reshape(8, 128).T),
        "lnb": np.ascontiguousarray(full["L3_attn_ln_b"].reshape(8, 128).T),
        "binu": np.ascontiguousarray(full["L3_attn_b_in"][:1024].reshape(8, 128).T),
        "binv": np.ascontiguousarray(full["L3_attn_b_in"][1024:].reshape(8, 128).T),
        "bh": np.ascontiguousarray(full["L3_attn_b_h"].reshape(8, 128).T),
    }

def build_l3rec(nsteps=4096, t_off=0):
    nc = bass.Bass("TRN2", target_bir_lowering=False, detect_race_conditions=False)
    stack = ExitStack()
    g_dd = nc.dram_tensor("g_all", [D, 4096], BF16, kind="ExternalInput")
    vb_dd = nc.dram_tensor("vb_all", [D, 4096], BF16, kind="ExternalInput")
    wh_d = nc.dram_tensor("whT", [D, D], F32, kind="ExternalInput")
    h0_d = nc.dram_tensor("h0col", [128, 8], F32, kind="ExternalInput")
    hs_d = nc.dram_tensor("hs", [D, nsteps], BF16, kind="ExternalOutput")
    hn_d = nc.dram_tensor("hN", [128, 8], F32, kind="ExternalOutput")
    with stack:
        ctx = Ctx(nc, stack)
        V, SC = ctx.V, ctx.SC
        gb = ctx.sbuf("gb", [128, DB * 4096], BF16)
        vbb = ctx.sbuf("vbb", [128, DB * 4096], BF16)
        wh = ctx.sbuf("wh", [128, DB * D], BF16)
        hf = ctx.sbuf("hf", [128, 8])
        hb = ctx.sbuf("hb", [128, 8], BF16)
        cb = ctx.sbuf("cb", [128, 8])
        ab = ctx.sbuf("ab", [128, 8])
        mrow = ctx.sbuf("mrow", [1, D])
        hsb = ctx.sbuf("hsb", [128, DB * 512], BF16)
        on11 = ctx.sbuf("on11", [1, 1])
        p0 = ctx.psum("p0", [128, 512]); p1 = ctx.psum("p1", [128, 512])
        p2 = ctx.psum("p2", [128, 8])
        ld, st, ldg = ctx.S("ld"), ctx.S("st"), ctx.S("ldg")
        sT = ctx.S("mT")
        with nc.Block():
            V.memset(on11[:, :], 1.0)
            ld.inc(nc.sync.dma_start(hf[:, :], h0_d[:, :]))
            ldg.inc(nc.gpsimd.dma_start(gb[:, :].rearrange("p (b t) -> p b t", b=DB),
                                        bass.AP(g_dd, t_off, [[4096, 128], [128 * 4096, DB], [1, 4096 - 0 if nsteps == 4096 else nsteps]])
                                        if False else
                                        bass.AP(g_dd, t_off, [[4096, 128], [128 * 4096, DB], [1, 4096]])))
            ldg.inc(nc.gpsimd.dma_start(vbb[:, :].rearrange("p (b t) -> p b t", b=DB),
                                        bass.AP(vb_dd, t_off, [[4096, 128], [128 * 4096, DB], [1, 4096]])))
            ldg.inc(nc.gpsimd.dma_start(wh[:, :].rearrange("p (b g) -> p b g", b=DB),
                                        bass.AP(wh_d, 0, [[D, 128], [128 * D, DB], [1, D]])))
            for e in (nc.tensor, nc.vector, nc.scalar):
                e.wait_ge(ld.h, ld.n)
                e.wait_ge(ldg.h, ldg.n)
            V.tensor_copy(hb[:, :], hf[:, :])
            ctx.wV(nc.tensor)
            for t in range(nsteps):
                for b in range(DB):
                    nc.tensor.matmul(p0[0:1, 0:512], hb[:, b:b + 1],
                                     wh[:, b * D: b * D + 512],
                                     start=(b == 0), stop=(b == DB - 1))
                for b in range(DB):
                    ii = nc.tensor.matmul(p1[0:1, 0:512], hb[:, b:b + 1],
                                          wh[:, b * D + 512: b * D + 1024],
                                          start=(b == 0), stop=(b == DB - 1))
                sT.inc(ii, 1)
                nc.scalar.wait_ge(sT.h, sT.n)
                nc.vector.wait_ge(sT.h, sT.n)
                SC.activation(mrow[0:1, 0:512], p0[0:1, 0:512], AF.Identity)
                V.tensor_copy(mrow[0:1, 512:1024], p1[0:1, 0:512])
                ctx.wS(nc.tensor)
                ctx.wV(nc.tensor)
                for b in range(DB):
                    ii = nc.tensor.matmul(p2[:, b:b + 1], mrow[0:1, b * 128:(b + 1) * 128],
                                          on11[0:1, 0:1], start=True, stop=True)
                sT.inc(ii, 1)
                nc.vector.wait_ge(sT.h, sT.n)
                vbt = bass.AP(vbb, t, [[DB * 4096, 128], [4096, DB]])
                gt = bass.AP(gb, t, [[DB * 4096, 128], [4096, DB]])
                V.tensor_add(ab[:, :], p2[:, 0:8], vbt)
                ctx.wV(nc.scalar)
                SC.activation(cb[:, :], ab[:, :], AF.Tanh)
                ctx.wS(nc.vector)
                V.tensor_sub(cb[:, :], cb[:, :], hf[:, :])
                V.tensor_mul(cb[:, :], cb[:, :], gt)
                V.tensor_add(hf[:, :], hf[:, :], cb[:, :])
                V.tensor_copy(hb[:, :], hf[:, :])
                hcol = bass.AP(hsb, t % 512, [[DB * 512, 128], [512, DB]])
                V.tensor_copy(hcol, hf[:, :])
                ctx.wV(nc.tensor)
                if t % 512 == 511:
                    ctx.wV(nc.sync)
                    st.inc(nc.sync.dma_start(
                        bass.AP(hs_d, t - 511, [[nsteps, 128], [128 * nsteps, DB], [1, 512]]),
                        sb3(hsb, 512)))
                    if t + 1 < nsteps:
                        nc.vector.wait_ge(st.h, st.n)
            ctx.wV(nc.sync)
            st.inc(nc.sync.dma_start(hn_d[:, :], hf[:, :]))
            nc.sync.wait_ge(st.h, st.n)
    return nc

def prep_l3rec(full, b, g_pair, vb_pair):
    g_all = np.concatenate([g_pair[0], g_pair[1]], axis=1)
    vb_all = np.concatenate([vb_pair[0], vb_pair[1]], axis=1)
    return {
        "g_all": np.ascontiguousarray(g_all),
        "vb_all": np.ascontiguousarray(vb_all),
        "whT": np.ascontiguousarray(full["L3_attn_W_h"].T),
        "h0col": np.zeros((128, 8), np.float32),
    }

def build_l3post():
    nc = bass.Bass("TRN2", target_bir_lowering=False, detect_race_conditions=False)
    stack = ExitStack()
    x_d = nc.dram_tensor("x", [D, T], F32, kind="ExternalInput")
    xn1_d = nc.dram_tensor("xn1", [D, T], BF16, kind="ExternalInput")
    hs_d = nc.dram_tensor("hs", [D, T], BF16, kind="ExternalInput")
    wo_d = nc.dram_tensor("woutT", [D, D], F32, kind="ExternalInput")
    bo_d = nc.dram_tensor("bout", [128, 8], F32, kind="ExternalInput")
    rt_d = nc.dram_tensor("routT", [D, 8], F32, kind="ExternalInput")
    eb_d = nc.dram_tensor("ebias", [8, 1], F32, kind="ExternalInput")
    i8_d = nc.dram_tensor("ident8", [8, 8], F32, kind="ExternalInput")
    i128_d = nc.dram_tensor("ident128", [128, 128], F32, kind="ExternalInput")
    we1_d = nc.dram_tensor("we1", [10 * D, 2048], F32, kind="ExternalInput")
    we3_d = nc.dram_tensor("we3", [10 * D, 2048], F32, kind="ExternalInput")
    we2_d = nc.dram_tensor("we2", [10 * 2048, D], F32, kind="ExternalInput")
    xo_d = nc.dram_tensor("xo", [D, T], F32, kind="ExternalOutput")
    cwscr = nc.dram_tensor("cwscr", [8, T], F32, kind="Internal")
    HID = 2048
    NHB = HID // 128
    with stack:
        ctx = Ctx(nc, stack)
        V, SC = ctx.V, ctx.SC
        he = ctx.sbuf("he", [128, NHB * T], BF16)
        x1 = ctx.sbuf("x1", [128, DB * T])
        xf = ctx.sbuf("xf", [128, DB * T], BF16)
        xrc = ctx.sbuf("xrc", [128, DB * 512])
        wo = xf[:, 0:DB * D]
        wes = [ctx.sbuf(f"wes{i}", [128, DB * 128], BF16) for i in range(2)]
        weh = ctx.sbuf("weh", [128, NHB * 128], BF16)
        bo = ctx.sbuf("bo_s", [128, 8])
        rt = ctx.sbuf("rt_s", [128, 8 * DB], BF16)
        ebs = ctx.sbuf("ebs_s", [8, 1])
        i8 = ctx.sbuf("i8_s", [8, 8])
        i128 = ctx.sbuf("i128_s", [128, 128])
        lg = xrc[0:8, 0:T]
        tl = ctx.sbuf("tl", [128, 8])
        ind = ctx.sbuf("ind", [128, 8])
        lrow = ctx.sbuf("lrow", [128, 4])
        cwb = ctx.sbuf("cwb", [128, 512])
        t1 = ctx.sbuf("t1b", [128, 512], BF16)
        t2 = ctx.sbuf("t2b", [128, 512], BF16)
        onesc = ctx.sbuf("onesc", [128, 1])
        ones1 = ctx.sbuf("ones1", [1, 128])
        p0 = ctx.psum("p0", [128, 512]); p1 = ctx.psum("p1", [128, 512])
        p2 = ctx.psum("p2", [128, 512]); p3 = ctx.psum("p3", [128, 512])
        ld, st, ldg = ctx.S("ld"), ctx.S("st"), ctx.S("ldg")
        sT = ctx.S("mT")
        with nc.Block():
            V.memset(onesc[:, :], 1.0)
            V.memset(ones1[:, :], 1.0)
            ld.inc(nc.sync.dma_start(he[:, 0:DB * T].rearrange("p (b t) -> p b t", b=DB),
                                     bass.AP(hs_d, 0, [[T, 128], [128 * T, DB], [1, T]])))
            ld.inc(nc.sync.dma_start(he[:, DB * T:2 * DB * T].rearrange("p (b t) -> p b t", b=DB),
                                     bass.AP(xn1_d, 0, [[T, 128], [128 * T, DB], [1, T]])))
            ldg.inc(nc.gpsimd.dma_start(wo.rearrange("p (b g) -> p b g", b=DB),
                                        bass.AP(wo_d, 0, [[D, 128], [128 * D, DB], [1, D]])))
            ldg.inc(nc.gpsimd.dma_start(rt[:, :].rearrange("p (b g) -> p b g", b=DB),
                                        bass.AP(rt_d, 0, [[8, 128], [128 * 8, DB], [1, 8]])))
            ld.inc(nc.sync.dma_start(bo[:, :], bo_d[:, :]))
            ld.inc(nc.sync.dma_start(ebs[:, :], eb_d[:, :]))
            ld.inc(nc.sync.dma_start(i8[:, :], i8_d[:, :]))
            ld.inc(nc.sync.dma_start(i128[:, :], i128_d[:, :]))
            for e in (nc.tensor, nc.vector, nc.scalar):
                e.wait_ge(ld.h, ld.n)
                e.wait_ge(ldg.h, ldg.n)
            ctx.wV(nc.tensor)
            cwst = ctx.sbuf("lnrb", [128, 1024])[0:1, 0:1024]
            cwrow = xrc[0:1, 2048:2048 + T]
            for tc in range(4):
                t0 = tc * 512
                ctx.wV(nc.sync)
                ld.inc(nc.sync.dma_start(xrc[:, :].rearrange("p (b t) -> p b t", b=DB),
                                         x_chunk_ap(x_d, t0, 512, T)))
                nc.vector.wait_ge(ld.h, ld.n)
                for dc in range(DB):
                    pt = (p0, p1)[dc % 2]
                    ctx.wV(nc.tensor)
                    for b in range(DB):
                        ii = nc.tensor.matmul(pt[:, 0:512],
                                              wo[:, b * D + dc * 128: b * D + dc * 128 + 128],
                                              he[:, b * T + t0: b * T + t0 + 512],
                                              start=(b == 0), stop=(b == DB - 1))
                    sT.inc(ii, 1)
                    nc.vector.wait_ge(sT.h, sT.n)
                    dst = x1[:, dc * T + t0: dc * T + t0 + 512]
                    V.tensor_scalar(dst, pt[:, 0:512], bo[:, dc:dc + 1], None, ALU.add)
                    V.tensor_add(dst, dst, xrc[:, dc * 512:(dc + 1) * 512])
                    V.tensor_add(dst, dst, he[:, DB * T + dc * T + t0: DB * T + dc * T + t0 + 512])
            emit_ln(ctx, x1, 0, T, xf, 0, T, 0, T)
            for tc in range(4):
                t0 = tc * 512
                ctx.wV(nc.tensor)
                for b in range(DB):
                    ii = nc.tensor.matmul(p2[0:8, 0:512], rt[:, b * 8:(b + 1) * 8],
                                          xf[:, b * T + t0: b * T + t0 + 512],
                                          start=(b == 0), stop=(b == DB - 1))
                sT.inc(ii, 1)
                nc.vector.wait_ge(sT.h, sT.n)
                V.tensor_scalar(lg[0:8, t0:t0 + 512], p2[0:8, 0:512], ebs[0:8, 0:1], None, ALU.add)
            for tt in range(16):
                ctx.wV(nc.tensor)
                ii = nc.tensor.matmul(p2[:, 0:8], lg[:, tt * 128:(tt + 1) * 128],
                                      i8[:, :], start=True, stop=True)
                sT.inc(ii, 1)
                nc.vector.wait_ge(sT.h, sT.n)
                V.tensor_copy(tl[:, :], p2[:, 0:8])
                V.reduce_max(lrow[:, 0:1], tl[:, :], axis=AX.X)
                V.tensor_scalar(ind[:, :], tl[:, :], lrow[:, 0:1], None, ALU.is_ge)
                V.scalar_tensor_tensor(tl[:, :], ind[:, :], -1e30, tl[:, :], ALU.mult, ALU.add)
                V.reduce_max(lrow[:, 1:2], tl[:, :], axis=AX.X)
                V.tensor_sub(lrow[:, 2:3], lrow[:, 0:1], lrow[:, 1:2])
                ctx.wV(nc.scalar)
                SC.activation(lrow[:, 2:3], lrow[:, 2:3], AF.Sigmoid)
                ctx.wS(nc.vector)
                V.tensor_scalar(lrow[:, 3:4], lrow[:, 2:3], -1.0, 1.0, ALU.mult, ALU.add)
                V.tensor_scalar(ind[:, :], ind[:, :], lrow[:, 2:3], None, ALU.mult)
                V.tensor_scalar(tl[:, :], tl[:, :], lrow[:, 1:2], None, ALU.is_ge)
                V.tensor_scalar(tl[:, :], tl[:, :], lrow[:, 3:4], None, ALU.mult)
                V.tensor_add(tl[:, :], tl[:, :], ind[:, :])
                ctx.wV(nc.tensor)
                nc.vector.wait_ge(st.h, st.n)
                for e in range(8):
                    ii = nc.tensor.matmul(p3[0:1, 0:128], tl[:, e:e + 1], i128[:, :],
                                          start=True, stop=True)
                    sT.inc(ii, 1)
                    nc.vector.wait_ge(sT.h, sT.n)
                    V.tensor_copy(cwst[:, e * 128:(e + 1) * 128], p3[0:1, 0:128])
                    ctx.wV(nc.tensor)
                ctx.wV(nc.sync)
                st.inc(nc.sync.dma_start(
                    bass.AP(cwscr, tt * 128, [[8 * T, 1], [T, 8], [1, 128]]),
                    bass.AP(ctx.bufs["lnrb"], 0, [[1024, 1], [128, 8], [1, 128]])))
            for e in range(10):
                routed = (e >= 2)
                for fc in range(NHB):
                    nc.gpsimd.wait_ge(sT.h, sT.n)
                    ldg.inc(nc.gpsimd.dma_start(wes[0][:, :].rearrange("p (b t) -> p b t", b=DB),
                                                bass.AP(we1_d, e * D * 2048 + fc * 128,
                                                        [[2048, 128], [128 * 2048, DB], [1, 128]])))
                    ldg.inc(nc.gpsimd.dma_start(wes[1][:, :].rearrange("p (b t) -> p b t", b=DB),
                                                bass.AP(we3_d, e * D * 2048 + fc * 128,
                                                        [[2048, 128], [128 * 2048, DB], [1, 128]])))
                    nc.tensor.wait_ge(ldg.h, ldg.n)
                    for tc in range(4):
                        t0 = tc * 512
                        ctx.wS(nc.tensor)
                        ctx.wV(nc.tensor)
                        for b in range(DB):
                            nc.tensor.matmul(p0[:, 0:512], wes[0][:, b * 128:(b + 1) * 128],
                                             xf[:, b * T + t0: b * T + t0 + 512],
                                             start=(b == 0), stop=(b == DB - 1))
                        for b in range(DB):
                            ii = nc.tensor.matmul(p1[:, 0:512], wes[1][:, b * 128:(b + 1) * 128],
                                                  xf[:, b * T + t0: b * T + t0 + 512],
                                                  start=(b == 0), stop=(b == DB - 1))
                        sT.inc(ii, 1)
                        nc.scalar.wait_ge(sT.h, sT.n)
                        SC.activation(t1[:, :], p0[:, 0:512], AF.Silu)
                        SC.activation(t2[:, :], p1[:, 0:512], AF.Identity)
                        ctx.wS(nc.vector)
                        V.tensor_mul(he[:, fc * T + t0: fc * T + t0 + 512], t1[:, :], t2[:, :])
                if routed:
                    ctx.wV(nc.sync)
                    nc.sync.wait_ge(st.h, st.n)
                    ld.inc(nc.sync.dma_start(cwrow, cwscr[e - 2:e - 1, :]))
                    nc.tensor.wait_ge(ld.h, ld.n)
                for dc in range(DB):
                    nc.gpsimd.wait_ge(sT.h, sT.n)
                    ldg.inc(nc.gpsimd.dma_start(weh[:, :].rearrange("p (b t) -> p b t", b=NHB),
                                                bass.AP(we2_d, e * 2048 * D + dc * 128,
                                                        [[D, 128], [128 * D, NHB], [1, 128]])))
                    nc.tensor.wait_ge(ldg.h, ldg.n)
                    for tc in range(4):
                        t0 = tc * 512
                        ctx.wV(nc.tensor)
                        for b in range(NHB):
                            ii = nc.tensor.matmul(p0[:, 0:512], weh[:, b * 128:(b + 1) * 128],
                                                  he[:, b * T + t0: b * T + t0 + 512],
                                                  start=(b == 0), stop=(b == NHB - 1))
                        if routed:
                            ii = nc.tensor.matmul(p2[:, 0:512], ones1[0:1, :],
                                                  xrc[0:1, 2048 + t0:2048 + t0 + 512],
                                                  start=True, stop=True)
                        sT.inc(ii, 1)
                        nc.vector.wait_ge(sT.h, sT.n)
                        dst = x1[:, dc * T + t0: dc * T + t0 + 512]
                        if routed:
                            V.tensor_copy(cwb[:, :], p2[:, 0:512])
                            V.tensor_mul(t1[:, :], p0[:, 0:512], cwb[:, :])
                            V.tensor_add(dst, dst, t1[:, :])
                        else:
                            V.tensor_add(dst, dst, p0[:, 0:512])
            ctx.wV(nc.sync)
            st.inc(nc.sync.dma_start(x_re(xo_d), sb3(x1, T)))
            nc.sync.wait_ge(st.h, st.n)
    return nc

def prep_l3post(full, b, r, hs_own):
    xc = full["x_cur"]
    xn1 = full["_xn1"][2 * b + r]
    we1 = np.concatenate([np.asarray(full["L3_ffn_sw1"]), np.asarray(full["L3_ffn_w1"])], 0)
    we3 = np.concatenate([np.asarray(full["L3_ffn_sw3"]), np.asarray(full["L3_ffn_w3"])], 0)
    we2 = np.concatenate([np.asarray(full["L3_ffn_sw2"]), np.asarray(full["L3_ffn_w2"])], 0)
    return {
        "x": np.ascontiguousarray(xc[b, r * T:(r + 1) * T, :].T),
        "xn1": np.ascontiguousarray(xn1),
        "hs": np.ascontiguousarray(hs_own),
        "woutT": np.ascontiguousarray(full["L3_attn_W_out"].T),
        "bout": np.ascontiguousarray(full["L3_attn_b_out"].reshape(8, 128).T),
        "routT": np.ascontiguousarray(full["L3_ffn_router_w"].T),
        "ebias": np.ascontiguousarray(full["L3_ffn_expert_bias"].reshape(8, 1)),
        "ident8": np.eye(8, dtype=np.float32),
        "ident128": np.eye(128, dtype=np.float32),
        "we1": np.ascontiguousarray(we1.transpose(0, 2, 1).reshape(10 * D, 2048)),
        "we3": np.ascontiguousarray(we3.transpose(0, 2, 1).reshape(10 * D, 2048)),
        "we2": np.ascontiguousarray(we2.transpose(0, 2, 1).reshape(10 * 2048, D)),
    }



def kernel(**inputs):
    full = dict(inputs)
    cores = list(range(8))
    full["x_cur"] = np.asarray(full["L0_attn_x"], np.float32)

    nc = build_l0a()
    res = run_bass_kernel_spmd(nc, [prep_l0a(full, c // 2, c % 2) for c in cores], core_ids=cores)
    full["x_cur"] = gather_x(res.results)

    nc = build_l0f()
    res = run_bass_kernel_spmd(nc, [prep_l0f(full, c // 2, c % 2) for c in cores], core_ids=cores)
    full["x_cur"] = gather_x(res.results)

    nc = build_l1kv()
    res = run_bass_kernel_spmd(nc, [prep_l1kv(full, c // 2, c % 2) for c in cores], core_ids=cores)
    kv = res.results
    nc = build_l1main()
    ims = []
    for c in cores:
        b, r = c // 2, c % 2
        qf_pair = {rr: np.asarray(kv[2 * b + rr]["qf"]) for rr in (0, 1)}
        kf_pair = {rr: np.asarray(kv[2 * b + rr]["kf"]) for rr in (0, 1)}
        vt_pair = {rr: np.asarray(kv[2 * b + rr]["vT"]) for rr in (0, 1)}
        ims.append(prep_l1main(full, b, r, qf_pair, kf_pair, vt_pair))
    res = run_bass_kernel_spmd(nc, ims, core_ids=cores)
    full["x_cur"] = gather_l1(res.results)

    nc = build_l2kv()
    res = run_bass_kernel_spmd(nc, [prep_l2kv(full, c // 2, c % 2) for c in cores], core_ids=cores)
    kv = res.results
    nc = build_l2main()
    ims = []
    for c in cores:
        b, r = c // 2, c % 2
        qt_pair = {rr: np.asarray(kv[2 * b + rr]["qt"]) for rr in (0, 1)}
        kt_pair = {rr: np.asarray(kv[2 * b + rr]["kt"]) for rr in (0, 1)}
        vt_pair = {rr: np.asarray(kv[2 * b + rr]["vT"]) for rr in (0, 1)}
        g1_pair = {rr: np.asarray(kv[2 * b + rr]["g1o"]) for rr in (0, 1)}
        ims.append(prep_l2main(full, b, r, qt_pair, kt_pair, vt_pair, g1_pair))
    res = run_bass_kernel_spmd(nc, ims, core_ids=cores)
    full["x_cur"] = gather_l1(res.results)

    nc = build_l2f()
    res = run_bass_kernel_spmd(nc, [prep_l2f(full, c // 2, c % 2) for c in cores], core_ids=cores)
    full["x_cur"] = gather_x(res.results)

    nc = build_l3pre()
    res = run_bass_kernel_spmd(nc, [prep_l3pre(full, c // 2, c % 2) for c in cores], core_ids=cores)
    pre = res.results
    nc = build_l3rec(nsteps=4096)
    ims = []
    for c in cores:
        b = c // 2
        g_pair = {rr: np.asarray(pre[2 * b + rr]["g"]) for rr in (0, 1)}
        vb_pair = {rr: np.asarray(pre[2 * b + rr]["vb"]) for rr in (0, 1)}
        ims.append(prep_l3rec(full, b, g_pair, vb_pair))
    res = run_bass_kernel_spmd(nc, ims, core_ids=cores)
    rec = res.results
    nc = build_l3post()
    full["_xn1"] = {c: np.asarray(pre[c]["xn1"]) for c in cores}
    ims = []
    for c in cores:
        b, r = c // 2, c % 2
        hs_all = np.asarray(rec[2 * b]["hs"])
        ims.append(prep_l3post(full, b, r, np.ascontiguousarray(hs_all[:, r * T:(r + 1) * T])))
    res = run_bass_kernel_spmd(nc, ims, core_ids=cores)
    return gather_x(res.results)
